# version 31
# baseline (speedup 1.0000x reference)
import numpy as np
import ml_dtypes
import concourse.bass as bass
import concourse.mybir as mybir
from concourse.bass_utils import run_bass_kernel_spmd

F32 = mybir.dt.float32
BF16 = mybir.dt.bfloat16
AF = mybir.ActivationFunctionType
ALU = mybir.AluOpType
AX = mybir.AxisListType
bfdt = ml_dtypes.bfloat16

T = 2048
D = 1024
NT = 16
NCH = 8
NPAIR = 8
HD = 64


def build_nc(debug=(), stop="full", ln_tiles=NT):
    STOP = {"ln1": 1, "vproj": 2, "pairs": 3, "oproj": 4, "ln2": 5,
            "weff": 6, "full": 9}[stop]
    nc = bass.Bass("TRN2", target_bir_lowering=False)

    x_d = nc.dram_tensor("x", [T, D], F32, kind="ExternalInput").ap()
    wq_d = nc.dram_tensor("wq", [D, D], F32, kind="ExternalInput").ap()
    wk_d = nc.dram_tensor("wk", [D, D], F32, kind="ExternalInput").ap()
    wv_d = nc.dram_tensor("wv", [D, D], F32, kind="ExternalInput").ap()
    wo_d = nc.dram_tensor("wo", [D, D], F32, kind="ExternalInput").ap()
    wt_d = nc.dram_tensor("wt", [D, D], F32, kind="ExternalInput").ap()
    bqk_d = nc.dram_tensor("bqk", [128, 16], F32, kind="ExternalInput").ap()
    brow_d = nc.dram_tensor("brow", [3, D], F32, kind="ExternalInput").ap()
    consts_d = nc.dram_tensor("consts", [128, 8], F32, kind="ExternalInput").ap()
    cbf_d = nc.dram_tensor("cbf", [128, 386], BF16, kind="ExternalInput").ap()
    ldt_d = nc.dram_tensor("ldt", [100, 4], F32, kind="ExternalInput").ap()
    lut_d = nc.dram_tensor("lut", [4, 50], F32, kind="ExternalInput").ap()
    upt_d = nc.dram_tensor("upt", [50, D], F32, kind="ExternalInput").ap()
    da_d = nc.dram_tensor("da", [100, D], F32, kind="ExternalInput").ap()
    y_d = nc.dram_tensor("y", [T, D], F32, kind="ExternalOutput").ap()

    import contextlib
    with contextlib.ExitStack() as _st:
        xs = _st.enter_context(nc.sbuf_tensor("xs", [128, NT * D], F32))
        xhT = _st.enter_context(nc.sbuf_tensor("xhT", [128, NCH * T], BF16))
        vtok = _st.enter_context(nc.sbuf_tensor("vtok", [128, NT * D], BF16))
        yT = _st.enter_context(nc.sbuf_tensor("yT", [128, NCH * T], BF16))
        qT = _st.enter_context(nc.sbuf_tensor("qT", [128, T], BF16))
        kT = _st.enter_context(nc.sbuf_tensor("kT", [128, T], BF16))
        pt = _st.enter_context(nc.sbuf_tensor("pt", [128, 4 * 512], BF16))
        wqs = _st.enter_context(nc.sbuf_tensor("wqs", [128, D], BF16))
        wks = _st.enter_context(nc.sbuf_tensor("wks", [128, D], BF16))
        wbuf = _st.enter_context(nc.sbuf_tensor("wbuf", [128, NCH * D], BF16))
        sqspan = _st.enter_context(nc.sbuf_tensor("sqspan", [128, 512], BF16))
        sqspan2 = _st.enter_context(nc.sbuf_tensor("sqspan2", [128, 512], BF16))
        brow3 = _st.enter_context(nc.sbuf_tensor("brow3", [65, D], BF16))
        rl = _st.enter_context(nc.sbuf_tensor("rl", [33, 512], F32))
        rlb = _st.enter_context(nc.sbuf_tensor("rlb", [128, 512], F32))
        stat = _st.enter_context(nc.sbuf_tensor("stat", [128, 64], F32))
        bqk = _st.enter_context(nc.sbuf_tensor("bqk_s", [128, 16], F32))
        consts = _st.enter_context(nc.sbuf_tensor("consts_s", [128, 8], F32))
        cbf = _st.enter_context(nc.sbuf_tensor("cbf_s", [128, 386], BF16))
        onesf = _st.enter_context(nc.sbuf_tensor("onesf", [33, 128], F32))
        qnrow = _st.enter_context(nc.sbuf_tensor("qnrow", [1, 64], F32))
        biasMC = _st.enter_context(nc.sbuf_tensor("biasMC", [128, 16], F32))
        pAB0 = _st.enter_context(nc.psum_tensor("pAB0", [128, 1024], F32))
        pAB1 = _st.enter_context(nc.psum_tensor("pAB1", [128, 1024], F32))
        pb4 = _st.enter_context(nc.psum_tensor("pb4", [128, 512], F32))
        pb5 = _st.enter_context(nc.psum_tensor("pb5", [128, 512], F32))
        pb6 = _st.enter_context(nc.psum_tensor("pb6", [128, 512], F32))
        pb7 = _st.enter_context(nc.psum_tensor("pb7", [128, 512], F32))
        sT = _st.enter_context(nc.semaphore("sT"))
        sS = _st.enter_context(nc.semaphore("sS"))
        sV = _st.enter_context(nc.semaphore("sV"))
        dW = _st.enter_context(nc.semaphore("dW"))
        dIn = _st.enter_context(nc.semaphore("dIn"))
        dSc = _st.enter_context(nc.semaphore("dSc"))
        dOut = _st.enter_context(nc.semaphore("dOut"))
        dX0 = _st.enter_context(nc.semaphore("dX0"))
        dX1 = _st.enter_context(nc.semaphore("dX1"))
        dX2 = _st.enter_context(nc.semaphore("dX2"))
        dX3 = _st.enter_context(nc.semaphore("dX3"))
        block = _st.enter_context(nc.Block())

        sems = {"sT": sT, "sS": sS, "sV": sV, "dW": dW, "dIn": dIn,
                "dSc": dSc, "dOut": dOut, "dX0": dX0, "dX1": dX1,
                "dX2": dX2, "dX3": dX3}
        own = {"tensor": "sT", "scalar": "sS", "vector": "sV"}
        plan = {e: [] for e in ("tensor", "vector", "scalar", "gpsimd", "sync")}
        cnt = {s: 0 for s in sems}
        lastw = {e: {s: 0 for s in sems} for e in plan}

        def emit(eng, fn, waits=(), inc=None, amt=1, selfw=False):
            waits = list(waits)
            if selfw and eng in own:
                waits.append((own[eng], cnt[own[eng]]))
            ws = []
            for s, v in waits:
                if v <= 0:
                    continue
                if own.get(eng) == s and not selfw:
                    continue
                if lastw[eng][s] >= v:
                    continue
                ws.append((s, v))
                lastw[eng][s] = v
            tok = None
            if inc is not None:
                cnt[inc] += amt
                tok = (inc, cnt[inc])
            plan[eng].append((ws, fn, inc, amt))
            return tok

        PB = [None, None, None, None, pb4, pb5, pb6, pb7]
        PAB = [pAB0, pAB1]
        C_I, C_R, C_ONE, C_E2 = 0, 128, 256, 384

        def tap(name, ap_fn, shape, dtype, waits):
            if name not in debug:
                return
            t = nc.dram_tensor("dbg_" + name, shape, dtype, kind="ExternalOutput").ap()
            emit("sync", lambda e, t=t: e.dma_start(out=t, in_=ap_fn()),
                 waits=waits, inc="dOut", amt=16)

        try:
            for wi in range(24):
                emit("tensor", lambda e, wi=wi: e.matmul(
                    PB[6 + wi % 2][:, 0:512], wbuf[0:128, 0:128],
                    wbuf[:, 1024:1536], start=True, stop=True), inc="sT")

            tok_x = []
            for g in range(4):
                src = x_d[512 * g:512 * (g + 1), :].rearrange("(i p) c -> p i c", p=128)
                dst = xs[:, 4096 * g:4096 * (g + 1)].rearrange("p (i c) -> p i c", i=4)
                tok_x.append(emit(
                    "sync", lambda e, src=src, dst=dst: e.dma_start(out=dst, in_=src),
                    inc="dX%d" % g, amt=16))
            tok_consts = emit("sync", lambda e: e.dma_start(out=consts[:, :], in_=consts_d),
                              inc="dIn", amt=16)
            tok_cbf = emit("sync", lambda e: e.dma_start(out=cbf[:, :], in_=cbf_d),
                           inc="dIn", amt=16)
            tok_bqk = emit("sync", lambda e: e.dma_start(out=bqk[:, :], in_=bqk_d),
                           inc="dIn", amt=16)
            tok_onesf = emit("vector", lambda e: e.memset(onesf[:, :], 1.0), inc="sV")

            for r in range(3):
                tok_brow = emit("gpsimd", lambda e, r=r: e.dma_start(
                    out=brow3[32 * r:32 * r + 1, :], in_=brow_d[r:r + 1, :]),
                    inc="dW", amt=16)
            tok_wv = []
            for a in range(NCH):
                tok_wv.append(emit(
                    "gpsimd", lambda e, a=a: e.dma_start(
                        out=wbuf[:, D * a:D * (a + 1)], in_=wv_d[128 * a:128 * (a + 1), :]),
                    inc="dW", amt=16))
            tok_wq = emit("gpsimd", lambda e: e.dma_start(out=wqs[:, :], in_=wq_d[0:128, :]),
                          inc="dW", amt=16)
            tok_wk = emit("gpsimd", lambda e: e.dma_start(out=wks[:, :], in_=wk_d[0:128, :]),
                          inc="dW", amt=16)

            class _Done(Exception):
                pass

            def finish_debug():
                emit("sync", lambda e: None, waits=[("dOut", cnt["dOut"])])
                raise _Done()

            def layernorm_round(rnd, after_tile=None):
                tr_tok = [None] * NT
                nt_r = ln_tiles if rnd == 0 else NT
                xc_tok = [None] * NT
                dg_tok = [None] * NT
                slot_free = {}
                tbank_read = {}
                for g in range((nt_r + 3) // 4):
                    tiles = [i for i in range(4 * g, min(4 * g + 4, nt_r))]
                    def sc(i):
                        return 16 * (i % 4)
                    if rnd == 0:
                        xw_all = [("dX%d" % g, 16), ("dIn", tok_bqk[1])]
                    else:
                        xw_all = [("sV", resid_tok[i][1]) for i in tiles]
                    for i in tiles:
                        wl = xw_all if rnd == 0 else [("sV", resid_tok[i][1])]
                        emit("vector", lambda e, i=i: e.bn_stats(
                            stat[:, sc(i):sc(i) + 6], xs[:, D * i:D * i + 512]),
                            waits=wl + [("dIn", tok_consts[1])], inc="sV")
                        emit("vector", lambda e, i=i: e.bn_stats(
                            stat[:, sc(i) + 6:sc(i) + 12],
                            xs[:, D * i + 512:D * i + 1024]), inc="sV")
                    for i in tiles:
                        emit("vector", lambda e, i=i: e.bn_aggr(
                            stat[:, sc(i) + 12:sc(i) + 14], stat[:, sc(i):sc(i) + 12]),
                            inc="sV", selfw=True)
                    for i in tiles:
                        emit("vector", lambda e, i=i: e.tensor_scalar(
                            stat[:, sc(i) + 3:sc(i) + 4], stat[:, sc(i) + 12:sc(i) + 13],
                            consts[:, 6:7], None, op0=ALU.mult), inc="sV", selfw=True)
                    t_v8 = None
                    for i in tiles:
                        emit("scalar", lambda e, i=i: e.activation(
                            stat[:, sc(i) + 9:sc(i) + 10], stat[:, sc(i) + 13:sc(i) + 14],
                            AF.Ln, bias=consts[:, 2:3], scale=1.0),
                            waits=[("sV", cnt["sV"])], inc="sS")
                    t_inv = None
                    for i in tiles:
                        t_inv = emit("scalar", lambda e, i=i: e.activation(
                            stat[:, sc(i) + 10:sc(i) + 11], stat[:, sc(i) + 9:sc(i) + 10],
                            AF.Exp, bias=consts[:, 5:6], scale=-0.5),
                            inc="sS", selfw=True)
                    for i in tiles:
                        nw = [("sS", t_inv[1]), ("sV", cnt["sV"])]
                        if rnd == 1:
                            nw.append(("sT", att_last_pv[1]))
                        xc_tok[i] = emit("scalar", lambda e, i=i: e.add(
                            vtok[:, 1024 * i:1024 * i + D], xs[:, D * i:D * (i + 1)],
                            stat[:, sc(i) + 3:sc(i) + 4]),
                            waits=nw, inc="sS", selfw=True)
                    for i in tiles:
                        dw = [("dIn", tok_cbf[1]), ("sS", t_inv[1])]
                        if (i % 4) in slot_free:
                            dw.append(slot_free[i % 4])
                        dg_tok[i] = emit("vector", lambda e, i=i: e.tensor_scalar(
                            sqspan[:, 128 * (i % 4):128 * (i % 4) + 128],
                            cbf[:, C_I:C_I + 128], stat[:, sc(i) + 10:sc(i) + 11],
                            None, op0=ALU.mult), waits=dw, inc="sV", selfw=True)
                    if rnd == 0 and 0 in tiles:
                        tap("stat0", lambda: stat[:, :], [128, 64], F32,
                            [("sV", dg_tok[0][1])])
                    for i in tiles:
                        mw0 = [xc_tok[i], ("sV", dg_tok[i][1])]
                        if rnd == 1:
                            mw0.append(att_sS_holder[0])
                        for a in range(NCH):
                            bnum = a % 4
                            bank = PAB[bnum // 2]
                            col = 512 * (bnum % 2) + 128 * (a // 4)
                            bw2 = list(mw0) if a == 0 else []
                            if bnum in tbank_read:
                                bw2.append(tbank_read[bnum])
                            emit("tensor", lambda e, i=i, a=a, bank=bank, col=col:
                                 e.matmul(
                                bank[:, col:col + 128],
                                vtok[:, 1024 * i + 128 * a:1024 * i + 128 * (a + 1)],
                                sqspan[:, 128 * (i % 4):128 * (i % 4) + 128],
                                start=True, stop=True), waits=bw2, inc="sT")
                            ew = [("sT", cnt["sT"])]
                            if rnd == 1:
                                ew.append(("sT", qkproj_last[1]))
                            if a < 4:
                                tr_tok[i] = emit("vector", lambda e, i=i, a=a, bank=bank,
                                                 col=col: e.tensor_copy(
                                    xhT[:, T * a + 128 * i: T * a + 128 * (i + 1)],
                                    bank[:, col:col + 128]), waits=ew, inc="sV")
                                tbank_read[bnum] = ("sV", tr_tok[i][1])
                            else:
                                tr_tok[i] = emit("scalar", lambda e, i=i, a=a, bank=bank,
                                                 col=col: e.copy(
                                    xhT[:, T * a + 128 * i: T * a + 128 * (i + 1)],
                                    bank[:, col:col + 128]), waits=ew, inc="sS")
                                tbank_read[bnum] = ("sS", tr_tok[i][1])
                        slot_free[i % 4] = ("sT", cnt["sT"])
                        tr_tok[i] = [("sS", cnt["sS"]), ("sV", cnt["sV"])]
                        if after_tile is not None:
                            after_tile(i, tr_tok[i])
                return tr_tok

            resid_tok = [None] * NT
            att_done_holder = [None]
            oproj_v_holder = [None]
            att_sS_holder = [None]
            att_last_pv = None
            qkproj_last = None

            vproj_tok = [None] * NT
            pbank_evac = {}
            CB = [6]
            ln_gate = [[]]
            chain_guard = [None]
            ln1_evac = [None]

            def vproj_tile(i, trtok):
                for h in range(2):
                    bnum = 6 + (2 * i + h) % 2
                    bank = PB[bnum]
                    bw = list(trtok) + [("dW", tok_wk[1]), ("dIn", tok_cbf[1])]
                    if bnum in pbank_evac:
                        bw.append(pbank_evac[bnum])
                    for a in range(NCH):
                        emit("tensor", lambda e, i=i, h=h, a=a, bank=bank: e.matmul(
                            bank[:, 0:512],
                            xhT[:, T * a + 128 * i: T * a + 128 * (i + 1)],
                            wbuf[:, D * a + 512 * h: D * a + 512 * (h + 1)],
                            start=(a == 0), stop=False),
                            waits=bw if a == 0 else (), inc="sT")
                    t_mm = emit("tensor", lambda e, h=h, bank=bank: e.matmul(
                        bank[:, 0:512], cbf[0:1, C_ONE:C_ONE + 128],
                        brow3[0:1, 512 * h:512 * (h + 1)],
                        start=False, stop=True), inc="sT")
                    vproj_tok[i] = emit("vector", lambda e, i=i, h=h, bank=bank:
                                        e.tensor_copy(
                        vtok[:, D * i + 512 * h: D * i + 512 * (h + 1)], bank[:, 0:512]),
                        waits=[("sT", t_mm[1])], inc="sV")
                    pbank_evac[bnum] = ("sV", vproj_tok[i][1])

            att_v_prev = [cnt["sV"]]
            pair_exp_last = [("sS", 0)]
            globals_ = {}
            KB = [kT, qT]
            pending_fin = [None]
            pending_pre = [None]
            fin_tok = [None, None]
            free_y = {}
            free_l = {}
            state = {}

            def emit_group(tp, tgt, stage, s, bcol, evmap):
                bnum = CB[0] + s % 2
                bank = PB[bnum]
                st = state[tp]
                bw = [st["wq"], st["wk"], ("dIn", tok_bqk[1]),
                      ("sV", att_v_prev[0])] + list(ln_gate[0])
                if bnum in pbank_evac:
                    bw.append(pbank_evac[bnum])
                for a in range(NCH):
                    emit("tensor", lambda e, s=s, a=a, bank=bank, stage=stage: e.matmul(
                        bank[:, 0:512],
                        stage[:, 128 * a:128 * (a + 1)],
                        xhT[:, T * a + 512 * s: T * a + 512 * (s + 1)],
                        start=(a == 0), stop=(a == NCH - 1)),
                        waits=bw if a == 0 else (), inc="sT")
                t_mm = ("sT", cnt["sT"])
                ev = emit("vector", lambda e, s=s, bank=bank, tgt=tgt, bcol=bcol:
                          e.tensor_scalar(tgt[:, 512 * s:512 * (s + 1)],
                                          bank[:, 0:512], bqk[:, bcol:bcol + 1],
                                          None, op0=ALU.add),
                          waits=[t_mm], inc="sV")
                pbank_evac[bnum] = ("sV", ev[1])
                evmap[s] = ("sV", ev[1])
                if tp == NPAIR - 1 and tgt is not kT and tgt is not qT and s == 3:
                    pass

            def emit_chain(tp, tgt, s, base, par):
                st = state[tp]
                sqb = sqspan[:, 0:512] if par == 0 else sqspan2[:, 0:512]
                evtok = (st["qevac"] if base == 0 else st["kevac"])[s]
                qb = 32 * (tp % 2)
                vw = [evtok]
                if chain_guard[0] is not None:
                    vw.append(chain_guard[0])
                if par in st["sq_prev"]:
                    vw.append(st["sq_prev"][par])
                t_sq = emit("vector", lambda e, tgt=tgt, s=s, sqb=sqb: e.tensor_mul(
                    sqb, tgt[:, 512 * s:512 * (s + 1)],
                    tgt[:, 512 * s:512 * (s + 1)]), waits=vw, inc="sV")
                for hh in range(2):
                    bnum = CB[0] + (s + hh) % 2
                    qbank = PB[bnum]
                    mw = [("sV", t_sq[1])]
                    if bnum in pbank_evac:
                        mw.append(pbank_evac[bnum])
                    t_mm = emit("tensor", lambda e, hh=hh, qbank=qbank, sqb=sqb:
                                e.matmul(
                        qbank[0:1, 0:512],
                        cbf[:, C_E2 + hh:C_E2 + hh + 1], sqb,
                        start=True, stop=True), waits=mw, inc="sT")
                    emit("vector", lambda e, base=base, hh=hh, s=s, qbank=qbank, qb=qb:
                         e.tensor_reduce(
                        qnrow[0:1, qb + base + 4 * hh + s: qb + base + 4 * hh + s + 1],
                        qbank[0:1, 0:512], axis=AX.X, op=ALU.max),
                        waits=[("sT", t_mm[1])], inc="sV")
                    pbank_evac[bnum] = ("sV", cnt["sV"])
                st["sq_prev"][par] = ("sT", cnt["sT"])

            def emit_knm(tp):
                qb = 32 * (tp % 2)
                for hh in range(2):
                    emit("vector", lambda e, hh=hh, qb=qb: e.tensor_reduce(
                        qnrow[0:1, qb + 16 + hh:qb + 17 + hh],
                        qnrow[0:1, qb + 8 + 4 * hh:qb + 12 + 4 * hh],
                        axis=AX.X, op=ALU.max), inc="sV", selfw=True)

            def emit_nm(tp, c):
                qb = 32 * (tp % 2)
                for hh in range(2):
                    emit("vector", lambda e, c=c, hh=hh, qb=qb: e.tensor_scalar(
                        qnrow[0:1, qb + 18 + 4 * hh + c:qb + 19 + 4 * hh + c],
                        qnrow[0:1, qb + 4 * hh + c:qb + 4 * hh + c + 1],
                        qnrow[0:1, qb + 16 + hh:qb + 17 + hh], None, op0=ALU.mult),
                        inc="sV", selfw=True)
                emit("vector", lambda e, c=c, qb=qb: e.tensor_max(
                    qnrow[0:1, qb + 26 + c:qb + 27 + c],
                    qnrow[0:1, qb + 18 + c:qb + 19 + c],
                    qnrow[0:1, qb + 22 + c:qb + 23 + c]), inc="sV", selfw=True)
                emit("scalar", lambda e, c=c, qb=qb: e.activation(
                    qnrow[0:1, qb + 26 + c:qb + 27 + c],
                    qnrow[0:1, qb + 26 + c:qb + 27 + c], AF.Ln,
                    bias=consts[0:1, 2:3], scale=1.0),
                    waits=[("sV", cnt["sV"])], inc="sS")
                t_s2 = emit("scalar", lambda e, c=c, qb=qb: e.activation(
                    qnrow[0:1, qb + 26 + c:qb + 27 + c],
                    qnrow[0:1, qb + 26 + c:qb + 27 + c], AF.Exp,
                    bias=consts[0:1, 5:6], scale=0.5), inc="sS", selfw=True)
                state[tp]["nm"][c] = emit("vector", lambda e, c=c, qb=qb: e.tensor_scalar(
                    qnrow[0:1, qb + 26 + c:qb + 27 + c],
                    qnrow[0:1, qb + 26 + c:qb + 27 + c],
                    consts[0:1, 3:4], None, op0=ALU.mult),
                    waits=[("sS", t_s2[1])], inc="sV", selfw=True)

            def emit_bcast(tp, c, boff):
                st = state[tp]
                qb = 32 * (tp % 2)
                bnum = CB[0] + boff
                bank = PB[bnum]
                bw = [st["nm"][c], ("sV", tok_onesf[1]), pair_exp_last[0]]
                if bnum in pbank_evac:
                    bw.append(pbank_evac[bnum])
                t_bc = emit("tensor", lambda e, c=c, bank=bank, qb=qb: e.matmul(
                    bank[:, 0:1], onesf[0:1, 0:128],
                    qnrow[0:1, qb + 26 + c:qb + 27 + c], start=True, stop=True),
                    waits=bw, inc="sT")
                e2 = emit("vector", lambda e, c=c, bank=bank, tp=tp: e.tensor_copy(
                    biasMC[:, 8 * (tp % 2) + c:8 * (tp % 2) + c + 1], bank[:, 0:1]),
                    waits=[("sT", t_bc[1])], inc="sV")
                pbank_evac[bnum] = ("sV", e2[1])
                st["bias"][c] = ("sV", e2[1])

            def new_state(tp):
                state[tp] = {
                    "wq": tok_wq, "wk": tok_wk, "kevac": {}, "qevac": {},
                    "nm": {}, "bias": {}, "sq_prev": {},
                    "kT": KB[tp % 2],
                    "qv": T * tp,
                }

            def boundary_steps(tp, with_groups=True):
                st = state[tp]
                ktgt = st["kT"]
                qtgt_base = st["qv"]
                qview = yT[:, qtgt_base:qtgt_base + T]
                steps = []
                if not with_groups:
                    steps.append(lambda: emit_chain(tp, ktgt, 0, 8, 0))
                    steps.append(lambda: emit_chain(tp, ktgt, 1, 8, 1))
                    steps.append(lambda: emit_chain(tp, ktgt, 2, 8, 0))
                    steps.append(lambda: (emit_chain(tp, ktgt, 3, 8, 1),
                                          emit_knm(tp)))
                    steps.append(lambda: (emit_chain(tp, qview, 0, 0, 0),
                                          emit_nm(tp, 0)))
                    steps.append(lambda: (emit_chain(tp, qview, 1, 0, 1),
                                          emit_nm(tp, 1)))
                    steps.append(lambda: (emit_chain(tp, qview, 2, 0, 0),
                                          emit_nm(tp, 2), emit_bcast(tp, 0, 0)))
                    steps.append(lambda: (emit_chain(tp, qview, 3, 0, 1),
                                          emit_nm(tp, 3), emit_bcast(tp, 1, 1),
                                          prefetch(tp)))
                    steps.append(lambda: emit_bcast(tp, 2, 0))
                    steps.append(lambda: emit_bcast(tp, 3, 1))
                    return steps
                steps.append(lambda: emit_group(tp, ktgt, wks, 0, 8 + tp, st["kevac"]))
                steps.append(lambda: (emit_group(tp, ktgt, wks, 1, 8 + tp, st["kevac"]),
                                      emit_chain(tp, ktgt, 0, 8, 0)))
                steps.append(lambda: (emit_group(tp, ktgt, wks, 2, 8 + tp, st["kevac"]),
                                      emit_chain(tp, ktgt, 1, 8, 1)))
                steps.append(lambda: (emit_group(tp, ktgt, wks, 3, 8 + tp, st["kevac"]),
                                      emit_chain(tp, ktgt, 2, 8, 0)))
                steps.append(lambda: (emit_chain(tp, ktgt, 3, 8, 1), emit_knm(tp)))
                steps.append(lambda: emit_group(tp, qview, wqs, 0, tp, st["qevac"]))
                steps.append(lambda: (emit_chain(tp, qview, 0, 0, 0), emit_nm(tp, 0)))
                steps.append(lambda: emit_group(tp, qview, wqs, 1, tp, st["qevac"]))
                steps.append(lambda: (emit_chain(tp, qview, 1, 0, 1), emit_nm(tp, 1)))
                steps.append(lambda: (emit_group(tp, qview, wqs, 2, tp, st["qevac"]),
                                      emit_bcast(tp, 0, 0)))
                steps.append(lambda: (emit_chain(tp, qview, 2, 0, 0), emit_nm(tp, 2)))
                steps.append(lambda: (emit_group(tp, qview, wqs, 3, tp, st["qevac"]),
                                      emit_bcast(tp, 1, 1)))
                steps.append(lambda: (emit_chain(tp, qview, 3, 0, 1), emit_nm(tp, 3),
                                      prefetch(tp)))
                steps.append(lambda: emit_bcast(tp, 2, 0))
                steps.append(lambda: emit_bcast(tp, 3, 1))
                if tp == NPAIR - 1:
                    steps.append(lambda: qk_last_mark())
                return steps

            def prefetch(tp):
                if tp < NPAIR - 1:
                    globals_['tok_wq'] = emit(
                        "gpsimd", lambda e, tp=tp: e.dma_start(
                            out=wqs[:, :],
                            in_=wq_d[128 * (tp + 1):128 * (tp + 2), :]),
                        waits=[("sT", cnt["sT"])], inc="dSc", amt=16)
                    globals_['tok_wk'] = emit(
                        "gpsimd", lambda e, tp=tp: e.dma_start(
                            out=wks[:, :],
                            in_=wk_d[128 * (tp + 1):128 * (tp + 2), :]),
                        inc="dSc", amt=16)

            qk_last = [None]

            def qk_last_mark():
                qk_last[0] = ("sT", cnt["sT"])


            new_state(0)
            p0_sched = {3: [("k", 0)], 5: [("q", 0)], 7: [("k", 1)],
                        9: [("q", 1)], 11: [("k", 2)], 13: [("q", 2)],
                        15: [("k", 3), ("q", 3)]}

            def ln1_cb(i, trtok):
                vproj_tile(i, trtok)
                ln_gate[0] = list(trtok)
                st0 = state[0]
                for kind, s in p0_sched.get(i, ()):
                    if kind == "k":
                        emit_group(0, st0["kT"], wks, s, 8, st0["kevac"])
                    else:
                        emit_group(0, yT[:, 0:T], wqs, s, 0, st0["qevac"])

            tr1 = layernorm_round(0, after_tile=ln1_cb)
            ln1_done = [("sV", cnt["sV"])]
            ln_gate[0] = list(ln1_done)
            chain_guard[0] = ("sT", cnt["sT"])
            ln1_evac[0] = [("sV", cnt["sV"]), ("sS", cnt["sS"])]
            tap("xht1", lambda: xhT[:, :], [128, NCH * T], BF16, list(ln1_done))
            if STOP < 2:
                finish_debug()
            tap("vtok", lambda: vtok[:, :], [128, NT * D], BF16, [("sV", cnt["sV"])])
            if STOP < 3:
                finish_debug()
            vproj_done = ("sV", cnt["sV"])

            tok_wo = []
            for a in range(NCH):
                tok_wo.append(emit(
                    "gpsimd", lambda e, a=a: e.dma_start(
                        out=wbuf[:, D * a:D * (a + 1)], in_=wo_d[128 * a:128 * (a + 1), :]),
                    waits=[("sT", cnt["sT"])], inc="dW", amt=16))

            for stp in boundary_steps(0, with_groups=False):
                stp()
            if 'tok_wq' in globals_:
                tok_wq = globals_.pop('tok_wq')
                tok_wk = globals_.pop('tok_wk')

            for p in range(NPAIR):
                hA, hB = 2 * p, 2 * p + 1
                st_p = state[p]
                kevac = st_p["kevac"]
                qevac = st_p["qevac"]
                bias_tok_c = st_p["bias"]
                kTp = st_p["kT"]
                qvb = st_p["qv"]
                bias_col = 8 * (p % 2)
                if p < NPAIR - 1:
                    new_state(p + 1)
                    bsteps = boundary_steps(p + 1)
                else:
                    bsteps = []
                bq_i = [0]

                def run_bstep(cb):
                    if bq_i[0] < len(bsteps):
                        CB[0] = cb
                        bsteps[bq_i[0]]()
                        bq_i[0] += 1

                exp_tok = {}
                free_bank_tok = {}
                free_pt_tok = {}
                for c in range(4):
                    jlast = 4 * c + 3
                    bY = PB[4 + 2 * (c % 2)]
                    bL = PB[5 + 2 * (c % 2)]

                    def emit_scores(j, c=c):
                        off = max(0, 128 * j - 512 * c)
                        w = 512 - off
                        qlo = qvb + 512 * c + off
                        diag = (j >= 4 * c)
                        pab = PAB[j % 2]
                        sw = [qevac[c], kevac[j // 4]]
                        if p == 0 and c == 0 and j <= 1 and ln1_evac[0]:
                            sw += ln1_evac[0]
                        if (j % 2) in free_bank_tok:
                            sw.append(free_bank_tok[j % 2])
                        emit("tensor", lambda e, j=j, qlo=qlo, w=w, pab=pab, diag=diag,
                             kTp=kTp: e.matmul(
                            pab[:, 0:w], kTp[0:64, 128 * j:128 * (j + 1)],
                            yT[0:64, qlo:qlo + w], start=True, stop=not diag,
                            tile_position=(0, 0)), waits=sw, inc="sT")
                        emit("tensor", lambda e, j=j, qlo=qlo, w=w, pab=pab, diag=diag,
                             kTp=kTp: e.matmul(
                            pab[:, 512:512 + w], kTp[64:128, 128 * j:128 * (j + 1)],
                            yT[64:128, qlo:qlo + w], start=True, stop=not diag,
                            tile_position=(64, 0)), inc="sT")
                        if diag:
                            emit("tensor", lambda e, pab=pab: e.matmul(
                                pab[:, 0:128], cbf[:, C_I:C_I + 128],
                                cbf[:, C_R:C_R + 128], start=False, stop=True), inc="sT")
                            emit("tensor", lambda e, pab=pab: e.matmul(
                                pab[:, 512:640], cbf[:, C_I:C_I + 128],
                                cbf[:, C_R:C_R + 128], start=False, stop=True), inc="sT")
                        t_scores = ("sT", cnt["sT"])
                        slot = j % 2
                        ew = [t_scores, bias_tok_c[c]]
                        if slot in free_pt_tok:
                            ew.append(free_pt_tok[slot])
                        eA = emit("scalar", lambda e, slot=slot, w=w, pab=pab, c=c,
                                  bias_col=bias_col: e.activation(
                            pt[:, 1024 * slot:1024 * slot + 512 + w], pab[:, 0:512 + w],
                            AF.Exp, bias=biasMC[:, bias_col + c:bias_col + c + 1],
                            scale=0.125), waits=ew, inc="sS")
                        free_bank_tok[j % 2] = ("sS", eA[1])
                        exp_tok[j] = ("sS", eA[1])

                    def emit_lpv(j, c=c, bY=bY, bL=bL, jlast=jlast):
                        off = max(0, 128 * j - 512 * c)
                        w = 512 - off
                        pA = 1024 * (j % 2)
                        pB = 1024 * (j % 2) + 512
                        lw = [exp_tok[j], vproj_done]
                        if j == 0:
                            if (c % 2) in free_l:
                                lw.append(free_l[c % 2])
                            if (c % 2) in free_y:
                                lw.append(free_y[c % 2])
                            for bn in (4 + 2 * (c % 2), 5 + 2 * (c % 2)):
                                if bn in pbank_evac:
                                    lw.append(pbank_evac.pop(bn))
                        emit("tensor", lambda e, j=j, off=off, w=w, pA=pA, bL=bL:
                             e.matmul(
                            bL[0:1, off:512], cbf[:, C_ONE:C_ONE + 1],
                            pt[:, pA:pA + w],
                            start=(j == 0), stop=(j == jlast), tile_position=(0, 0)),
                            waits=lw, inc="sT")
                        emit("tensor", lambda e, j=j, off=off, w=w, pB=pB, bL=bL:
                             e.matmul(
                            bL[32:33, off:512], cbf[:, C_ONE:C_ONE + 1],
                            pt[:, pB:pB + w],
                            start=(j == 0), stop=(j == jlast), tile_position=(0, 32)),
                            inc="sT")
                        emit("tensor", lambda e, j=j, off=off, w=w, pA=pA, hA=hA,
                             bY=bY: e.matmul(
                            bY[0:64, off:512],
                            vtok[:, D * j + HD * hA: D * j + HD * hA + HD],
                            pt[:, pA:pA + w],
                            start=(j == 0), stop=(j == jlast), tile_position=(0, 0)),
                            inc="sT")
                        t_pv = emit("tensor", lambda e, j=j, off=off, w=w, pB=pB,
                                    hB=hB, bY=bY: e.matmul(
                            bY[64:128, off:512],
                            vtok[:, D * j + HD * hB: D * j + HD * hB + HD],
                            pt[:, pB:pB + w],
                            start=(j == 0), stop=(j == jlast), tile_position=(0, 64)),
                            inc="sT")
                        free_pt_tok[j % 2] = ("sT", t_pv[1])

                    for j in range(jlast + 2):
                        if j <= jlast:
                            emit_scores(j)
                        if pending_pre[0] is not None and j == 1:
                            pending_pre[0]()
                            pending_pre[0] = None
                        if pending_fin[0] is not None and j == (2 if c == 0 else 4):
                            pending_fin[0]()
                            pending_fin[0] = None
                        if ((c == 0 and j in (3, 4))
                                or (c == 1 and j in (5, 7))
                                or (c == 2 and j in (5, 7, 9, 11))
                                or (c == 3 and j in (5, 7, 9, 11, 13, 15, 16))):
                            run_bstep(6 if c % 2 == 0 else 4)
                        if j >= 1:
                            emit_lpv(j - 1)
                    att_last_pv = ("sT", cnt["sT"])
                    pv_last_c = att_last_pv

                    def fin_pre(c=c, bL=bL, pv_last_c=pv_last_c):
                        t_l1 = emit("vector", lambda e, bL=bL: e.tensor_copy(
                            rl[0:33, 0:512], bL[0:33, 0:512]), waits=[pv_last_c],
                            inc="sV")
                        emit("scalar", lambda e: e.activation(
                            rl[0:33, 0:512], rl[0:33, 0:512], AF.Ln,
                            bias=consts[0:33, 5:6], scale=1.0),
                            waits=[("sV", t_l1[1])], inc="sS")
                        t_rcp = emit("scalar", lambda e: e.activation(
                            rl[0:33, 0:512], rl[0:33, 0:512], AF.Exp,
                            bias=consts[0:33, 5:6], scale=-1.0), inc="sS", selfw=True)
                        fin_tok[0] = ("sS", t_rcp[1])
                        fin_tok[1] = ("sV", t_l1[1])

                    def fin(c=c, bY=bY, bL=bL, pv_last_c=pv_last_c, p=p):
                        bLn = 5 + 2 * (c % 2)
                        bYn = 4 + 2 * (c % 2)
                        rw = [fin_tok[0], fin_tok[1]]
                        if bLn in pbank_evac:
                            rw.append(pbank_evac[bLn])
                        emit("tensor", lambda e, bL=bL: e.matmul(
                            bL[0:64, 0:512], onesf[0:1, 0:64], rl[0:1, 0:512],
                            start=True, stop=True, tile_position=(0, 0)),
                            waits=rw, inc="sT")
                        t_rlb = emit("tensor", lambda e, bL=bL: e.matmul(
                            bL[64:128, 0:512], onesf[32:33, 0:64], rl[32:33, 0:512],
                            start=True, stop=True, tile_position=(32, 64)), inc="sT")
                        t_rlbe = emit("vector", lambda e, bL=bL: e.tensor_copy(
                            rlb[:, 0:512], bL[:, 0:512]), waits=[("sT", t_rlb[1])],
                            inc="sV")
                        t_ye = emit("vector", lambda e, p=p, c=c, bY=bY: e.tensor_mul(
                            yT[:, T * p + 512 * c: T * p + 512 * (c + 1)],
                            bY[:, 0:512], rlb[:, 0:512]),
                            waits=[pv_last_c], inc="sV", selfw=True)
                        pbank_evac[bLn] = ("sV", t_rlbe[1])
                        pbank_evac[bYn] = ("sV", t_ye[1])
                        free_l[c % 2] = ("sV", t_rlbe[1])
                        free_y[c % 2] = ("sV", t_ye[1])
                    pending_pre[0] = fin_pre
                    pending_fin[0] = fin
                while bq_i[0] < len(bsteps):
                    run_bstep(4)
                if 'tok_wq' in globals_:
                    tok_wq = globals_.pop('tok_wq')
                    tok_wk = globals_.pop('tok_wk')
                att_v_prev[0] = cnt["sV"]
                pair_exp_last[0] = ("sS", cnt["sS"])
            if pending_pre[0] is not None:
                pending_pre[0]()
                pending_pre[0] = None
            if pending_fin[0] is not None:
                pending_fin[0]()
                pending_fin[0] = None
            qkproj_last = qk_last[0]
            att_done_v = ("sV", cnt["sV"])
            att_done_t = ("sT", cnt["sT"])
            att_done_holder[0] = att_done_v
            att_sS_holder[0] = ("sS", cnt["sS"])
            tap("yt", lambda: yT[:, :], [128, NCH * T], BF16, [att_done_v])
            if STOP < 4:
                finish_debug()

            for i in range(NT):
                for h in range(2):
                    bnum = 6 + (2 * i + h) % 2
                    bank = PB[bnum]
                    bw = [att_done_v, ("dW", tok_wo[7][1])]
                    if bnum in pbank_evac:
                        bw.append(pbank_evac[bnum])
                    for a in range(NCH):
                        emit("tensor", lambda e, i=i, h=h, a=a, bank=bank: e.matmul(
                            bank[:, 0:512],
                            yT[:, T * a + 128 * i: T * a + 128 * (i + 1)],
                            wbuf[:, D * a + 512 * h: D * a + 512 * (h + 1)],
                            start=(a == 0), stop=False),
                            waits=bw if a == 0 else (), inc="sT")
                    t_mm = emit("tensor", lambda e, h=h, bank=bank: e.matmul(
                        bank[:, 0:512], cbf[32:33, C_ONE:C_ONE + 128],
                        brow3[32:33, 512 * h:512 * (h + 1)],
                        start=False, stop=True), inc="sT")
                    resid_tok[i] = emit("vector", lambda e, i=i, h=h, bank=bank:
                                        e.scalar_tensor_tensor(
                        xs[:, D * i + 512 * h: D * i + 512 * (h + 1)], bank[:, 0:512],
                        consts[:, 4:5], xs[:, D * i + 512 * h: D * i + 512 * (h + 1)],
                        op0=ALU.mult, op1=ALU.add), waits=[("sT", t_mm[1])], inc="sV")
                    pbank_evac[bnum] = ("sV", resid_tok[i][1])
            oproj_done_t = ("sT", cnt["sT"])
            oproj_v_holder[0] = ("sV", cnt["sV"])
            tap("xso", lambda: xs[:, :], [128, NT * D], F32,
                [("sV", resid_tok[NT - 1][1])])
            if STOP < 5:
                finish_debug()

            tok_wt = []
            for a in range(NCH):
                tok_wt.append(emit(
                    "gpsimd", lambda e, a=a: e.dma_start(
                        out=wbuf[:, D * a:D * (a + 1)], in_=wt_d[128 * a:128 * (a + 1), :]),
                    waits=[oproj_done_t], inc="dW", amt=16))
            tok_da = emit("gpsimd", lambda e: e.dma_start(out=qT[0:100, 0:1024], in_=da_d),
                          waits=[att_done_t, qkproj_last], inc="dW", amt=16)
            tok_upt = emit("gpsimd", lambda e: e.dma_start(out=qT[0:50, 1024:2048], in_=upt_d),
                           inc="dW", amt=16)
            tok_ldt = emit("gpsimd", lambda e: e.dma_start(out=pt[0:100, 0:4], in_=ldt_d),
                           inc="dW", amt=16)
            tok_lut = emit("gpsimd", lambda e: e.dma_start(out=pt[0:4, 4:54], in_=lut_d),
                           inc="dW", amt=16)

            tr2 = layernorm_round(1)
            ln2_done = [("sS", cnt["sS"]), ("sV", cnt["sV"])]
            tap("xht2", lambda: xhT[:, :], [128, NCH * T], BF16, list(ln2_done))

            for h in range(2):
                emit("tensor", lambda e, h=h: e.matmul(
                    pb6[0:4, 0:512], pt[0:100, 0:4], qT[0:100, 512 * h:512 * (h + 1)],
                    start=True, stop=True),
                    waits=[("dW", tok_lut[1]), ("sV", cnt["sV"])], inc="sT")
                emit("vector", lambda e, h=h: e.tensor_copy(
                    kT[0:4, 512 * h:512 * (h + 1)], pb6[0:4, 0:512]),
                    waits=[("sT", cnt["sT"])], inc="sV")
            for h in range(2):
                emit("tensor", lambda e, h=h: e.matmul(
                    pb6[0:50, 0:512], pt[0:4, 4:54], kT[0:4, 512 * h:512 * (h + 1)],
                    start=True, stop=True), waits=[("sV", cnt["sV"])], inc="sT")
                emit("vector", lambda e, h=h: e.tensor_copy(
                    kT[0:50, 1024 + 512 * h:1024 + 512 * (h + 1)], pb6[0:50, 0:512]),
                    waits=[("sT", cnt["sT"])], inc="sV")
            wefft_tok = [None] * NCH
            a2_tok = ("sV", cnt["sV"])
            for a in range(NCH):
                for h in range(2):
                    bank = PB[6 + (2 * a + h) % 2]
                    emit("tensor", lambda e, a=a, h=h, bank=bank: e.matmul(
                        bank[:, 0:512], kT[0:50, 1024 + 128 * a:1024 + 128 * (a + 1)],
                        qT[0:50, 1024 + 512 * h:1024 + 512 * (h + 1)],
                        start=True, stop=True),
                        waits=[a2_tok, ("sV", cnt["sV"])], inc="sT")
                    wefft_tok[a] = emit("vector", lambda e, a=a, h=h, bank=bank: e.tensor_add(
                        yT[:, D * a + 512 * h: D * a + 512 * (h + 1)], bank[:, 0:512],
                        wbuf[:, D * a + 512 * h: D * a + 512 * (h + 1)]),
                        waits=[("sT", cnt["sT"]), ("dW", tok_lut[1]), att_done_v],
                        inc="sV")

            tap("weff", lambda: yT[:, 0:NCH * D], [128, NCH * D], BF16,
                [("sV", wefft_tok[7][1])])

            ffn_resid = [None] * NT
            for i in range(NT):
                for h in range(2):
                    bnum = 6 + (2 * i + h) % 2
                    bank = PB[bnum]
                    bw = list(tr2[i]) + [("sV", wefft_tok[7][1])]
                    if bnum in pbank_evac:
                        bw.append(pbank_evac[bnum])
                    for a in range(NCH):
                        emit("tensor", lambda e, i=i, h=h, a=a, bank=bank: e.matmul(
                            bank[:, 0:512],
                            xhT[:, T * a + 128 * i: T * a + 128 * (i + 1)],
                            yT[:, D * a + 512 * h: D * a + 512 * (h + 1)],
                            start=(a == 0), stop=False),
                            waits=bw if a == 0 else (), inc="sT")
                    t_mm = emit("tensor", lambda e, h=h, bank=bank: e.matmul(
                        bank[:, 0:512], cbf[64:65, C_ONE:C_ONE + 128],
                        brow3[64:65, 512 * h:512 * (h + 1)],
                        start=False, stop=True), inc="sT")
                    ffn_resid[i] = emit("vector", lambda e, i=i, h=h, bank=bank:
                                        e.scalar_tensor_tensor(
                        xs[:, D * i + 512 * h: D * i + 512 * (h + 1)], bank[:, 0:512],
                        consts[:, 4:5], xs[:, D * i + 512 * h: D * i + 512 * (h + 1)],
                        op0=ALU.mult, op1=ALU.add), waits=[("sT", t_mm[1])], inc="sV")
                    pbank_evac[bnum] = ("sV", ffn_resid[i][1])

            for g in range(4):
                dst = y_d[512 * g:512 * (g + 1), :].rearrange("(i p) c -> p i c", p=128)
                src = xs[:, 4096 * g:4096 * (g + 1)].rearrange("p (i c) -> p i c", i=4)
                ws = [("sV", ffn_resid[4 * g + k][1]) for k in range(4)]
                emit("sync", lambda e, src=src, dst=dst: e.dma_start(out=dst, in_=src),
                     waits=ws, inc="dOut", amt=16)
            emit("sync", lambda e, v=cnt["dOut"] + 64: None, waits=[("dOut", cnt["dOut"])])

        except _Done:
            pass
        def replay(eng_obj, name):
            for ws, fn, inc, amt in plan[name]:
                for s, v in ws:
                    eng_obj.wait_ge(sems[s], v)
                ins = fn(eng_obj)
                if inc is not None:
                    ins.then_inc(sems[inc], amt)

        @block.tensor
        def _(t):
            replay(t, "tensor")

        @block.vector
        def _(v):
            replay(v, "vector")

        @block.scalar
        def _(s):
            replay(s, "scalar")

        @block.gpsimd
        def _(g):
            replay(g, "gpsimd")

        @block.sync
        def _(y):
            replay(y, "sync")

    return nc


def kernel(L0_attn_x, L0_attn_Wq, L0_attn_bq, L0_attn_Wk, L0_attn_bk,
           L0_attn_Wv, L0_attn_bv, L0_attn_Wo, L0_attn_bo,
           L0_ffn_W, L0_ffn_b, L0_ffn_lora_down, L0_ffn_lora_up,
           L0_ffn_down_aux, L0_ffn_up_aux):
    x = np.ascontiguousarray(np.asarray(L0_attn_x, np.float32))
    Wq = np.asarray(L0_attn_Wq, np.float32)
    Wk = np.asarray(L0_attn_Wk, np.float32)

    def dev_layout(W):
        return np.ascontiguousarray(
            W.reshape(8, 128, 8, 128).transpose(2, 1, 0, 3).reshape(1024, 1024))

    wq_dev = dev_layout(Wq)
    wk_dev = dev_layout(Wk)
    wv = np.ascontiguousarray(np.asarray(L0_attn_Wv, np.float32))
    wo = np.ascontiguousarray(np.asarray(L0_attn_Wo, np.float32))
    wt = np.ascontiguousarray(np.asarray(L0_ffn_W, np.float32).T)
    bq = np.asarray(L0_attn_bq, np.float32)
    bk = np.asarray(L0_attn_bk, np.float32)
    bqk = np.ascontiguousarray(np.concatenate(
        [bq.reshape(8, 128).T, bk.reshape(8, 128).T], axis=1))
    brow = np.ascontiguousarray(np.stack(
        [np.asarray(L0_attn_bv, np.float32), np.asarray(L0_attn_bo, np.float32),
         np.asarray(L0_ffn_b, np.float32)], axis=0))
    consts = np.zeros((128, 8), np.float32)
    consts[:, 0] = 1.0 / 1024
    consts[:, 1] = -1.0 / 1024
    consts[:, 2] = 1e-5
    consts[:, 3] = -0.125
    consts[:, 4] = 1.0
    consts[:, 5] = 0.0
    consts[:, 6] = -1.0
    idx = np.arange(128)
    cbf = np.zeros((128, 386), np.float32)
    cbf[idx, idx] = 1.0
    cbf[:, 128:256] = np.where(idx[:, None] > idx[None, :], -30000.0, 0.0)
    cbf[:, 256:384] = 1.0
    cbf[idx < 64, 384] = 1.0
    cbf[idx >= 64, 385] = 1.0
    cbf = cbf.astype(bfdt)
    ldt = np.ascontiguousarray(np.asarray(L0_ffn_lora_down, np.float32).T)
    lut = np.ascontiguousarray(np.asarray(L0_ffn_lora_up, np.float32).T)
    upt = np.ascontiguousarray(np.asarray(L0_ffn_up_aux, np.float32).T)
    da = np.ascontiguousarray(np.asarray(L0_ffn_down_aux, np.float32))

    shared = {"wq": wq_dev, "wk": wk_dev, "wv": wv, "wo": wo, "wt": wt,
              "bqk": bqk, "brow": brow, "consts": consts, "cbf": cbf,
              "ldt": ldt, "lut": lut, "upt": upt, "da": da}
    in_maps = [dict(shared, x=x[core]) for core in range(8)]

    nc = build_nc()
    res = run_bass_kernel_spmd(nc, in_maps, core_ids=list(range(8)))
    out = np.stack([np.asarray(res.results[i]["y"]) for i in range(8)], axis=0)
    return out.astype(np.float32)


# version 32
# speedup vs baseline: 1.0576x; 1.0576x over previous
import numpy as np
import ml_dtypes
import concourse.bass as bass
import concourse.mybir as mybir
from concourse.bass_utils import run_bass_kernel_spmd

F32 = mybir.dt.float32
BF16 = mybir.dt.bfloat16
AF = mybir.ActivationFunctionType
ALU = mybir.AluOpType
AX = mybir.AxisListType
bfdt = ml_dtypes.bfloat16

T = 2048
D = 1024
NT = 16
NCH = 8
NPAIR = 8
HD = 64


def build_nc(debug=(), stop="full", ln_tiles=NT):
    STOP = {"ln1": 1, "vproj": 2, "pairs": 3, "oproj": 4, "ln2": 5,
            "weff": 6, "full": 9}[stop]
    nc = bass.Bass("TRN2", target_bir_lowering=False)

    x_d = nc.dram_tensor("x", [T, D], F32, kind="ExternalInput").ap()
    wq_d = nc.dram_tensor("wq", [D, D], F32, kind="ExternalInput").ap()
    wk_d = nc.dram_tensor("wk", [D, D], F32, kind="ExternalInput").ap()
    wv_d = nc.dram_tensor("wv", [D, D], F32, kind="ExternalInput").ap()
    wo_d = nc.dram_tensor("wo", [D, D], F32, kind="ExternalInput").ap()
    wt_d = nc.dram_tensor("wt", [D, D], F32, kind="ExternalInput").ap()
    bqk_d = nc.dram_tensor("bqk", [128, 16], F32, kind="ExternalInput").ap()
    brow_d = nc.dram_tensor("brow", [3, D], F32, kind="ExternalInput").ap()
    consts_d = nc.dram_tensor("consts", [128, 8], F32, kind="ExternalInput").ap()
    cbf_d = nc.dram_tensor("cbf", [128, 386], BF16, kind="ExternalInput").ap()
    ldt_d = nc.dram_tensor("ldt", [100, 4], F32, kind="ExternalInput").ap()
    lut_d = nc.dram_tensor("lut", [4, 50], F32, kind="ExternalInput").ap()
    upt_d = nc.dram_tensor("upt", [50, D], F32, kind="ExternalInput").ap()
    da_d = nc.dram_tensor("da", [100, D], F32, kind="ExternalInput").ap()
    y_d = nc.dram_tensor("y", [T, D], F32, kind="ExternalOutput").ap()

    import contextlib
    with contextlib.ExitStack() as _st:
        xs = _st.enter_context(nc.sbuf_tensor("xs", [128, NT * D], F32))
        xhT = _st.enter_context(nc.sbuf_tensor("xhT", [128, NCH * T], BF16))
        vtok = _st.enter_context(nc.sbuf_tensor("vtok", [128, NT * D], BF16))
        yT = _st.enter_context(nc.sbuf_tensor("yT", [128, NCH * T], BF16))
        qT = _st.enter_context(nc.sbuf_tensor("qT", [128, T], BF16))
        kT = _st.enter_context(nc.sbuf_tensor("kT", [128, T], BF16))
        pt = _st.enter_context(nc.sbuf_tensor("pt", [128, 4 * 512], BF16))
        wqs = _st.enter_context(nc.sbuf_tensor("wqs", [128, D], BF16))
        wks = _st.enter_context(nc.sbuf_tensor("wks", [128, D], BF16))
        wbuf = _st.enter_context(nc.sbuf_tensor("wbuf", [128, NCH * D], BF16))
        sqspan = _st.enter_context(nc.sbuf_tensor("sqspan", [128, 512], BF16))
        sqspan2 = _st.enter_context(nc.sbuf_tensor("sqspan2", [128, 512], BF16))
        brow3 = _st.enter_context(nc.sbuf_tensor("brow3", [65, D], BF16))
        rl = _st.enter_context(nc.sbuf_tensor("rl", [33, 512], F32))
        rlb = _st.enter_context(nc.sbuf_tensor("rlb", [128, 512], F32))
        stat = _st.enter_context(nc.sbuf_tensor("stat", [128, 128], F32))
        bqk = _st.enter_context(nc.sbuf_tensor("bqk_s", [128, 16], F32))
        consts = _st.enter_context(nc.sbuf_tensor("consts_s", [128, 8], F32))
        cbf = _st.enter_context(nc.sbuf_tensor("cbf_s", [128, 386], BF16))
        onesf = _st.enter_context(nc.sbuf_tensor("onesf", [33, 128], F32))
        qnrow = _st.enter_context(nc.sbuf_tensor("qnrow", [1, 64], F32))
        biasMC = _st.enter_context(nc.sbuf_tensor("biasMC", [128, 16], F32))
        pAB0 = _st.enter_context(nc.psum_tensor("pAB0", [128, 1024], F32))
        pAB1 = _st.enter_context(nc.psum_tensor("pAB1", [128, 1024], F32))
        pb4 = _st.enter_context(nc.psum_tensor("pb4", [128, 512], F32))
        pb5 = _st.enter_context(nc.psum_tensor("pb5", [128, 512], F32))
        pb6 = _st.enter_context(nc.psum_tensor("pb6", [128, 512], F32))
        pb7 = _st.enter_context(nc.psum_tensor("pb7", [128, 512], F32))
        sT = _st.enter_context(nc.semaphore("sT"))
        sS = _st.enter_context(nc.semaphore("sS"))
        sV = _st.enter_context(nc.semaphore("sV"))
        dW = _st.enter_context(nc.semaphore("dW"))
        dIn = _st.enter_context(nc.semaphore("dIn"))
        dSc = _st.enter_context(nc.semaphore("dSc"))
        dOut = _st.enter_context(nc.semaphore("dOut"))
        dX0 = _st.enter_context(nc.semaphore("dX0"))
        dX1 = _st.enter_context(nc.semaphore("dX1"))
        dX2 = _st.enter_context(nc.semaphore("dX2"))
        dX3 = _st.enter_context(nc.semaphore("dX3"))
        block = _st.enter_context(nc.Block())

        sems = {"sT": sT, "sS": sS, "sV": sV, "dW": dW, "dIn": dIn,
                "dSc": dSc, "dOut": dOut, "dX0": dX0, "dX1": dX1,
                "dX2": dX2, "dX3": dX3}
        own = {"tensor": "sT", "scalar": "sS", "vector": "sV"}
        plan = {e: [] for e in ("tensor", "vector", "scalar", "gpsimd", "sync")}
        cnt = {s: 0 for s in sems}
        lastw = {e: {s: 0 for s in sems} for e in plan}

        def emit(eng, fn, waits=(), inc=None, amt=1, selfw=False):
            waits = list(waits)
            if selfw and eng in own:
                waits.append((own[eng], cnt[own[eng]]))
            ws = []
            for s, v in waits:
                if v <= 0:
                    continue
                if own.get(eng) == s and not selfw:
                    continue
                if lastw[eng][s] >= v:
                    continue
                ws.append((s, v))
                lastw[eng][s] = v
            tok = None
            if inc is not None:
                cnt[inc] += amt
                tok = (inc, cnt[inc])
            plan[eng].append((ws, fn, inc, amt))
            return tok

        PB = [None, None, None, None, pb4, pb5, pb6, pb7]
        PAB = [pAB0, pAB1]
        C_I, C_R, C_ONE, C_E2 = 0, 128, 256, 384

        def tap(name, ap_fn, shape, dtype, waits):
            if name not in debug:
                return
            t = nc.dram_tensor("dbg_" + name, shape, dtype, kind="ExternalOutput").ap()
            emit("sync", lambda e, t=t: e.dma_start(out=t, in_=ap_fn()),
                 waits=waits, inc="dOut", amt=16)

        try:
            for wi in range(24):
                emit("tensor", lambda e, wi=wi: e.matmul(
                    PB[6 + wi % 2][:, 0:512], wbuf[0:128, 0:128],
                    wbuf[:, 1024:1536], start=True, stop=True), inc="sT")

            tok_consts = emit("sync", lambda e: e.dma_start(out=consts[:, :], in_=consts_d),
                              inc="dIn", amt=16)
            tok_cbf = emit("sync", lambda e: e.dma_start(out=cbf[:, :], in_=cbf_d),
                           inc="dIn", amt=16)
            tok_bqk = emit("sync", lambda e: e.dma_start(out=bqk[:, :], in_=bqk_d),
                           inc="dIn", amt=16)
            tok_onesf = emit("vector", lambda e: e.memset(onesf[:, :], 1.0), inc="sV")
            tok_x = []
            for g in range(4):
                for t4 in range(4):
                    i = 4 * g + t4
                    xw = [("dX%d" % (g - 1), 64)] if (t4 == 0 and g > 0) else ()
                    tok_x.append(emit(
                        "sync", lambda e, i=i: e.dma_start(
                            out=xs[:, 1024 * i:1024 * (i + 1)],
                            in_=x_d[128 * i:128 * (i + 1), :]),
                        waits=xw, inc="dX%d" % g, amt=16))

            tok_wv = []
            for a in range(NCH):
                tok_wv.append(emit(
                    "gpsimd", lambda e, a=a: e.dma_start(
                        out=wbuf[:, D * a:D * (a + 1)], in_=wv_d[128 * a:128 * (a + 1), :]),
                    waits=[("dX0", 64)] if a == 0 else (), inc="dW", amt=16))
            for r in range(3):
                tok_brow = emit("gpsimd", lambda e, r=r: e.dma_start(
                    out=brow3[32 * r:32 * r + 1, :], in_=brow_d[r:r + 1, :]),
                    inc="dW", amt=16)
            tok_wk = emit("gpsimd", lambda e: e.dma_start(out=wks[:, :], in_=wk_d[0:128, :]),
                          waits=[("dW", tok_wv[-1][1])], inc="dW", amt=16)
            tok_wq = emit("gpsimd", lambda e: e.dma_start(out=wqs[:, :], in_=wq_d[0:128, :]),
                          inc="dW", amt=16)

            class _Done(Exception):
                pass

            def finish_debug():
                emit("sync", lambda e: None, waits=[("dOut", cnt["dOut"])])
                raise _Done()

            def layernorm_round(rnd, after_tile=None):
                tr_tok = [None] * NT
                nt_r = ln_tiles if rnd == 0 else NT
                xc_tok = [None] * NT
                dg_tok = [None] * NT
                slot_free = {}
                tbank_read = {}
                t_inv = {}

                def sc(i):
                    return 16 * (i % 8)

                groups = [list(range(4 * g, min(4 * g + 4, nt_r)))
                          for g in range((nt_r + 3) // 4)]

                t_nmd = {}

                def stageA_V(g):
                    tiles = groups[g]
                    for i in tiles:
                        if rnd == 0:
                            wl = [("dX%d" % g, 16 * (i % 4 + 1)),
                                  ("dIn", tok_bqk[1])]
                        else:
                            wl = [("sV", resid_tok[i][1])]
                        emit("vector", lambda e, i=i: e.bn_stats(
                            stat[:, sc(i):sc(i) + 6], xs[:, D * i:D * i + 512]),
                            waits=wl + [("dIn", tok_consts[1])], inc="sV")
                        emit("vector", lambda e, i=i: e.bn_stats(
                            stat[:, sc(i) + 6:sc(i) + 12],
                            xs[:, D * i + 512:D * i + 1024]), inc="sV")
                    for i in tiles:
                        emit("vector", lambda e, i=i: e.bn_aggr(
                            stat[:, sc(i) + 12:sc(i) + 14], stat[:, sc(i):sc(i) + 12]),
                            inc="sV", selfw=True)
                    for i in tiles:
                        emit("vector", lambda e, i=i: e.tensor_scalar(
                            stat[:, sc(i) + 3:sc(i) + 4], stat[:, sc(i) + 12:sc(i) + 13],
                            consts[:, 6:7], None, op0=ALU.mult), inc="sV", selfw=True)
                    t_nmd[g] = ("sV", cnt["sV"])

                def stageA_S(g):
                    tiles = groups[g]
                    for i in tiles:
                        emit("scalar", lambda e, i=i: e.activation(
                            stat[:, sc(i) + 9:sc(i) + 10], stat[:, sc(i) + 13:sc(i) + 14],
                            AF.Ln, bias=consts[:, 2:3], scale=1.0),
                            waits=[t_nmd[g]], inc="sS")
                    for i in tiles:
                        t_inv[g] = emit("scalar", lambda e, i=i: e.activation(
                            stat[:, sc(i) + 10:sc(i) + 11], stat[:, sc(i) + 9:sc(i) + 10],
                            AF.Exp, bias=consts[:, 5:6], scale=-0.5),
                            inc="sS", selfw=True)
                    for i in tiles:
                        nw = [("sS", t_inv[g][1])]
                        if rnd == 1:
                            nw.append(("sT", att_last_pv[1]))
                        xc_tok[i] = emit("scalar", lambda e, i=i: e.add(
                            vtok[:, 1024 * i:1024 * i + D], xs[:, D * i:D * (i + 1)],
                            stat[:, sc(i) + 3:sc(i) + 4]),
                            waits=nw, inc="sS", selfw=True)

                pending_at = [None]

                def flush_at():
                    if pending_at[0] is not None and after_tile is not None:
                        after_tile(*pending_at[0])
                    pending_at[0] = None

                def stageB(g):
                    tiles = groups[g]
                    tg = t_inv[g]
                    for i in tiles:
                        dw = [("dIn", tok_cbf[1]), ("sS", tg[1])]
                        if (i % 4) in slot_free:
                            dw.append(slot_free[i % 4])
                        dg_tok[i] = emit("vector", lambda e, i=i: e.tensor_scalar(
                            sqspan[:, 128 * (i % 4):128 * (i % 4) + 128],
                            cbf[:, C_I:C_I + 128], stat[:, sc(i) + 10:sc(i) + 11],
                            None, op0=ALU.mult), waits=dw, inc="sV", selfw=True)
                    if rnd == 0 and 0 in tiles:
                        tap("stat0", lambda: stat[:, :], [128, 128], F32,
                            [("sV", dg_tok[0][1])])
                    for i in tiles:
                        flush_at()
                        mw0 = [xc_tok[i], ("sV", dg_tok[i][1])]
                        if rnd == 1:
                            mw0.append(att_sS_holder[0])
                        for a in range(NCH):
                            bnum = a % 4
                            bank = PAB[bnum // 2]
                            col = 512 * (bnum % 2) + 128 * (a // 4)
                            bw2 = list(mw0) if a == 0 else []
                            if bnum in tbank_read:
                                bw2.append(tbank_read[bnum])
                            emit("tensor", lambda e, i=i, a=a, bank=bank, col=col:
                                 e.matmul(
                                bank[:, col:col + 128],
                                vtok[:, 1024 * i + 128 * a:1024 * i + 128 * (a + 1)],
                                sqspan[:, 128 * (i % 4):128 * (i % 4) + 128],
                                start=True, stop=True), waits=bw2, inc="sT")
                            ew = [("sT", cnt["sT"])]
                            if rnd == 1:
                                ew.append(("sT", qkproj_last[1]))
                            if a < 4:
                                tr_tok[i] = emit("vector", lambda e, i=i, a=a, bank=bank,
                                                 col=col: e.tensor_copy(
                                    xhT[:, T * a + 128 * i: T * a + 128 * (i + 1)],
                                    bank[:, col:col + 128]), waits=ew, inc="sV")
                                tbank_read[bnum] = ("sV", tr_tok[i][1])
                            else:
                                tr_tok[i] = emit("scalar", lambda e, i=i, a=a, bank=bank,
                                                 col=col: e.copy(
                                    xhT[:, T * a + 128 * i: T * a + 128 * (i + 1)],
                                    bank[:, col:col + 128]), waits=ew, inc="sS")
                                tbank_read[bnum] = ("sS", tr_tok[i][1])
                        slot_free[i % 4] = ("sT", cnt["sT"])
                        tr_tok[i] = [("sS", cnt["sS"]), ("sV", cnt["sV"])]
                        pending_at[0] = (i, tr_tok[i])

                if groups:
                    stageA_V(0)
                    stageA_S(0)
                    for g in range(1, len(groups)):
                        stageA_V(g)
                        stageB(g - 1)
                        stageA_S(g)
                    stageB(len(groups) - 1)
                    flush_at()
                return tr_tok

            resid_tok = [None] * NT
            att_done_holder = [None]
            oproj_v_holder = [None]
            att_sS_holder = [None]
            att_last_pv = None
            qkproj_last = None

            vproj_tok = [None] * NT
            pbank_evac = {}
            CB = [6]
            ln_gate = [[]]
            chain_guard = [None]
            ln1_evac = [None]

            def vproj_tile(i, trtok):
                for h in range(2):
                    bnum = 6 + (2 * i + h) % 2
                    bank = PB[bnum]
                    bw = list(trtok) + [("dW", tok_wk[1]), ("dIn", tok_cbf[1])]
                    if bnum in pbank_evac:
                        bw.append(pbank_evac[bnum])
                    for a in range(NCH):
                        emit("tensor", lambda e, i=i, h=h, a=a, bank=bank: e.matmul(
                            bank[:, 0:512],
                            xhT[:, T * a + 128 * i: T * a + 128 * (i + 1)],
                            wbuf[:, D * a + 512 * h: D * a + 512 * (h + 1)],
                            start=(a == 0), stop=False),
                            waits=bw if a == 0 else (), inc="sT")
                    t_mm = emit("tensor", lambda e, h=h, bank=bank: e.matmul(
                        bank[:, 0:512], cbf[0:1, C_ONE:C_ONE + 128],
                        brow3[0:1, 512 * h:512 * (h + 1)],
                        start=False, stop=True), inc="sT")
                    vproj_tok[i] = emit("vector", lambda e, i=i, h=h, bank=bank:
                                        e.tensor_copy(
                        vtok[:, D * i + 512 * h: D * i + 512 * (h + 1)], bank[:, 0:512]),
                        waits=[("sT", t_mm[1])], inc="sV")
                    pbank_evac[bnum] = ("sV", vproj_tok[i][1])

            att_v_prev = [cnt["sV"]]
            pair_exp_last = [("sS", 0)]
            globals_ = {}
            KB = [kT, qT]
            pending_fin = [None]
            pending_pre = [None]
            fin_tok = [None, None]
            free_y = {}
            free_l = {}
            state = {}

            def emit_group(tp, tgt, stage, s, bcol, evmap):
                bnum = CB[0] + s % 2
                bank = PB[bnum]
                st = state[tp]
                bw = [st["wq"], st["wk"], ("dIn", tok_bqk[1]),
                      ("sV", att_v_prev[0])] + list(ln_gate[0])
                if bnum in pbank_evac:
                    bw.append(pbank_evac[bnum])
                for a in range(NCH):
                    emit("tensor", lambda e, s=s, a=a, bank=bank, stage=stage: e.matmul(
                        bank[:, 0:512],
                        stage[:, 128 * a:128 * (a + 1)],
                        xhT[:, T * a + 512 * s: T * a + 512 * (s + 1)],
                        start=(a == 0), stop=(a == NCH - 1)),
                        waits=bw if a == 0 else (), inc="sT")
                t_mm = ("sT", cnt["sT"])
                ev = emit("vector", lambda e, s=s, bank=bank, tgt=tgt, bcol=bcol:
                          e.tensor_scalar(tgt[:, 512 * s:512 * (s + 1)],
                                          bank[:, 0:512], bqk[:, bcol:bcol + 1],
                                          None, op0=ALU.add),
                          waits=[t_mm], inc="sV")
                pbank_evac[bnum] = ("sV", ev[1])
                evmap[s] = ("sV", ev[1])
                if tp == NPAIR - 1 and tgt is not kT and tgt is not qT and s == 3:
                    pass

            def emit_chain(tp, tgt, s, base, par):
                st = state[tp]
                sqb = sqspan[:, 0:512] if par == 0 else sqspan2[:, 0:512]
                evtok = (st["qevac"] if base == 0 else st["kevac"])[s]
                qb = 32 * (tp % 2)
                vw = [evtok]
                if chain_guard[0] is not None:
                    vw.append(chain_guard[0])
                if par in st["sq_prev"]:
                    vw.append(st["sq_prev"][par])
                t_sq = emit("vector", lambda e, tgt=tgt, s=s, sqb=sqb: e.tensor_mul(
                    sqb, tgt[:, 512 * s:512 * (s + 1)],
                    tgt[:, 512 * s:512 * (s + 1)]), waits=vw, inc="sV")
                for hh in range(2):
                    bnum = CB[0] + (s + hh) % 2
                    qbank = PB[bnum]
                    mw = [("sV", t_sq[1])]
                    if bnum in pbank_evac:
                        mw.append(pbank_evac[bnum])
                    t_mm = emit("tensor", lambda e, hh=hh, qbank=qbank, sqb=sqb:
                                e.matmul(
                        qbank[0:1, 0:512],
                        cbf[:, C_E2 + hh:C_E2 + hh + 1], sqb,
                        start=True, stop=True), waits=mw, inc="sT")
                    emit("vector", lambda e, base=base, hh=hh, s=s, qbank=qbank, qb=qb:
                         e.tensor_reduce(
                        qnrow[0:1, qb + base + 4 * hh + s: qb + base + 4 * hh + s + 1],
                        qbank[0:1, 0:512], axis=AX.X, op=ALU.max),
                        waits=[("sT", t_mm[1])], inc="sV")
                    pbank_evac[bnum] = ("sV", cnt["sV"])
                st["sq_prev"][par] = ("sT", cnt["sT"])

            def emit_knm(tp):
                qb = 32 * (tp % 2)
                for hh in range(2):
                    emit("vector", lambda e, hh=hh, qb=qb: e.tensor_reduce(
                        qnrow[0:1, qb + 16 + hh:qb + 17 + hh],
                        qnrow[0:1, qb + 8 + 4 * hh:qb + 12 + 4 * hh],
                        axis=AX.X, op=ALU.max), inc="sV", selfw=True)

            def emit_nm(tp, c):
                qb = 32 * (tp % 2)
                for hh in range(2):
                    emit("vector", lambda e, c=c, hh=hh, qb=qb: e.tensor_scalar(
                        qnrow[0:1, qb + 18 + 4 * hh + c:qb + 19 + 4 * hh + c],
                        qnrow[0:1, qb + 4 * hh + c:qb + 4 * hh + c + 1],
                        qnrow[0:1, qb + 16 + hh:qb + 17 + hh], None, op0=ALU.mult),
                        inc="sV", selfw=True)
                emit("vector", lambda e, c=c, qb=qb: e.tensor_max(
                    qnrow[0:1, qb + 26 + c:qb + 27 + c],
                    qnrow[0:1, qb + 18 + c:qb + 19 + c],
                    qnrow[0:1, qb + 22 + c:qb + 23 + c]), inc="sV", selfw=True)
                emit("scalar", lambda e, c=c, qb=qb: e.activation(
                    qnrow[0:1, qb + 26 + c:qb + 27 + c],
                    qnrow[0:1, qb + 26 + c:qb + 27 + c], AF.Ln,
                    bias=consts[0:1, 2:3], scale=1.0),
                    waits=[("sV", cnt["sV"])], inc="sS")
                t_s2 = emit("scalar", lambda e, c=c, qb=qb: e.activation(
                    qnrow[0:1, qb + 26 + c:qb + 27 + c],
                    qnrow[0:1, qb + 26 + c:qb + 27 + c], AF.Exp,
                    bias=consts[0:1, 5:6], scale=0.5), inc="sS", selfw=True)
                state[tp]["nm"][c] = emit("vector", lambda e, c=c, qb=qb: e.tensor_scalar(
                    qnrow[0:1, qb + 26 + c:qb + 27 + c],
                    qnrow[0:1, qb + 26 + c:qb + 27 + c],
                    consts[0:1, 3:4], None, op0=ALU.mult),
                    waits=[("sS", t_s2[1])], inc="sV", selfw=True)

            def emit_bcast(tp, c, boff):
                st = state[tp]
                qb = 32 * (tp % 2)
                bnum = CB[0] + boff
                bank = PB[bnum]
                bw = [st["nm"][c], ("sV", tok_onesf[1]), pair_exp_last[0]]
                if bnum in pbank_evac:
                    bw.append(pbank_evac[bnum])
                t_bc = emit("tensor", lambda e, c=c, bank=bank, qb=qb: e.matmul(
                    bank[:, 0:1], onesf[0:1, 0:128],
                    qnrow[0:1, qb + 26 + c:qb + 27 + c], start=True, stop=True),
                    waits=bw, inc="sT")
                e2 = emit("vector", lambda e, c=c, bank=bank, tp=tp: e.tensor_copy(
                    biasMC[:, 8 * (tp % 2) + c:8 * (tp % 2) + c + 1], bank[:, 0:1]),
                    waits=[("sT", t_bc[1])], inc="sV")
                pbank_evac[bnum] = ("sV", e2[1])
                st["bias"][c] = ("sV", e2[1])

            def new_state(tp):
                state[tp] = {
                    "wq": tok_wq, "wk": tok_wk, "kevac": {}, "qevac": {},
                    "nm": {}, "bias": {}, "sq_prev": {},
                    "kT": KB[tp % 2],
                    "qv": T * tp,
                }

            def boundary_steps(tp, with_groups=True):
                st = state[tp]
                ktgt = st["kT"]
                qtgt_base = st["qv"]
                qview = yT[:, qtgt_base:qtgt_base + T]
                steps = []
                if not with_groups:
                    steps.append(lambda: emit_chain(tp, ktgt, 0, 8, 0))
                    steps.append(lambda: emit_chain(tp, ktgt, 1, 8, 1))
                    steps.append(lambda: emit_chain(tp, ktgt, 2, 8, 0))
                    steps.append(lambda: (emit_chain(tp, ktgt, 3, 8, 1),
                                          emit_knm(tp)))
                    steps.append(lambda: (emit_chain(tp, qview, 0, 0, 0),
                                          emit_nm(tp, 0)))
                    steps.append(lambda: (emit_chain(tp, qview, 1, 0, 1),
                                          emit_nm(tp, 1)))
                    steps.append(lambda: (emit_chain(tp, qview, 2, 0, 0),
                                          emit_nm(tp, 2), emit_bcast(tp, 0, 0)))
                    steps.append(lambda: (emit_chain(tp, qview, 3, 0, 1),
                                          emit_nm(tp, 3), emit_bcast(tp, 1, 1),
                                          prefetch(tp)))
                    steps.append(lambda: emit_bcast(tp, 2, 0))
                    steps.append(lambda: emit_bcast(tp, 3, 1))
                    return steps
                steps.append(lambda: emit_group(tp, ktgt, wks, 0, 8 + tp, st["kevac"]))
                steps.append(lambda: (emit_group(tp, ktgt, wks, 1, 8 + tp, st["kevac"]),
                                      emit_chain(tp, ktgt, 0, 8, 0)))
                steps.append(lambda: (emit_group(tp, ktgt, wks, 2, 8 + tp, st["kevac"]),
                                      emit_chain(tp, ktgt, 1, 8, 1)))
                steps.append(lambda: (emit_group(tp, ktgt, wks, 3, 8 + tp, st["kevac"]),
                                      emit_chain(tp, ktgt, 2, 8, 0)))
                steps.append(lambda: (emit_chain(tp, ktgt, 3, 8, 1), emit_knm(tp)))
                steps.append(lambda: emit_group(tp, qview, wqs, 0, tp, st["qevac"]))
                steps.append(lambda: (emit_chain(tp, qview, 0, 0, 0), emit_nm(tp, 0)))
                steps.append(lambda: emit_group(tp, qview, wqs, 1, tp, st["qevac"]))
                steps.append(lambda: (emit_chain(tp, qview, 1, 0, 1), emit_nm(tp, 1)))
                steps.append(lambda: (emit_group(tp, qview, wqs, 2, tp, st["qevac"]),
                                      emit_bcast(tp, 0, 0)))
                steps.append(lambda: (emit_chain(tp, qview, 2, 0, 0), emit_nm(tp, 2)))
                steps.append(lambda: (emit_group(tp, qview, wqs, 3, tp, st["qevac"]),
                                      emit_bcast(tp, 1, 1)))
                steps.append(lambda: (emit_chain(tp, qview, 3, 0, 1), emit_nm(tp, 3),
                                      prefetch(tp)))
                steps.append(lambda: emit_bcast(tp, 2, 0))
                steps.append(lambda: emit_bcast(tp, 3, 1))
                if tp == NPAIR - 1:
                    steps.append(lambda: qk_last_mark())
                return steps

            def prefetch(tp):
                if tp < NPAIR - 1:
                    globals_['tok_wq'] = emit(
                        "gpsimd", lambda e, tp=tp: e.dma_start(
                            out=wqs[:, :],
                            in_=wq_d[128 * (tp + 1):128 * (tp + 2), :]),
                        waits=[("sT", cnt["sT"])], inc="dSc", amt=16)
                    globals_['tok_wk'] = emit(
                        "gpsimd", lambda e, tp=tp: e.dma_start(
                            out=wks[:, :],
                            in_=wk_d[128 * (tp + 1):128 * (tp + 2), :]),
                        inc="dSc", amt=16)

            qk_last = [None]

            def qk_last_mark():
                qk_last[0] = ("sT", cnt["sT"])


            new_state(0)
            p0_sched = {3: [("k", 0)], 5: [("q", 0)], 7: [("k", 1)],
                        9: [("q", 1)], 11: [("k", 2)], 13: [("q", 2)],
                        15: [("k", 3), ("q", 3)]}

            def ln1_cb(i, trtok):
                vproj_tile(i, trtok)
                ln_gate[0] = list(trtok)
                st0 = state[0]
                for kind, s in p0_sched.get(i, ()):
                    if kind == "k":
                        emit_group(0, st0["kT"], wks, s, 8, st0["kevac"])
                    else:
                        emit_group(0, yT[:, 0:T], wqs, s, 0, st0["qevac"])

            tr1 = layernorm_round(0, after_tile=ln1_cb)
            ln1_done = [("sV", cnt["sV"])]
            ln_gate[0] = list(ln1_done)
            chain_guard[0] = ("sT", cnt["sT"])
            ln1_evac[0] = [("sV", cnt["sV"]), ("sS", cnt["sS"])]
            tap("xht1", lambda: xhT[:, :], [128, NCH * T], BF16, list(ln1_done))
            if STOP < 2:
                finish_debug()
            tap("vtok", lambda: vtok[:, :], [128, NT * D], BF16, [("sV", cnt["sV"])])
            if STOP < 3:
                finish_debug()
            vproj_done = ("sV", cnt["sV"])

            tok_wo = []
            for a in range(NCH):
                tok_wo.append(emit(
                    "gpsimd", lambda e, a=a: e.dma_start(
                        out=wbuf[:, D * a:D * (a + 1)], in_=wo_d[128 * a:128 * (a + 1), :]),
                    waits=[("sT", cnt["sT"])], inc="dW", amt=16))

            for stp in boundary_steps(0, with_groups=False):
                stp()
            if 'tok_wq' in globals_:
                tok_wq = globals_.pop('tok_wq')
                tok_wk = globals_.pop('tok_wk')

            for p in range(NPAIR):
                hA, hB = 2 * p, 2 * p + 1
                st_p = state[p]
                kevac = st_p["kevac"]
                qevac = st_p["qevac"]
                bias_tok_c = st_p["bias"]
                kTp = st_p["kT"]
                qvb = st_p["qv"]
                bias_col = 8 * (p % 2)
                if p < NPAIR - 1:
                    new_state(p + 1)
                    bsteps = boundary_steps(p + 1)
                else:
                    bsteps = []
                bq_i = [0]

                def run_bstep(cb):
                    if bq_i[0] < len(bsteps):
                        CB[0] = cb
                        bsteps[bq_i[0]]()
                        bq_i[0] += 1

                exp_tok = {}
                free_bank_tok = {}
                free_pt_tok = {}
                for c in range(4):
                    jlast = 4 * c + 3
                    bY = PB[4 + 2 * (c % 2)]
                    bL = PB[5 + 2 * (c % 2)]

                    def emit_scores(j, c=c):
                        off = max(0, 128 * j - 512 * c)
                        w = 512 - off
                        qlo = qvb + 512 * c + off
                        diag = (j >= 4 * c)
                        pab = PAB[j % 2]
                        sw = [qevac[c], kevac[j // 4]]
                        if p == 0 and c == 0 and j <= 1 and ln1_evac[0]:
                            sw += ln1_evac[0]
                        if (j % 2) in free_bank_tok:
                            sw.append(free_bank_tok[j % 2])
                        emit("tensor", lambda e, j=j, qlo=qlo, w=w, pab=pab, diag=diag,
                             kTp=kTp: e.matmul(
                            pab[:, 0:w], kTp[0:64, 128 * j:128 * (j + 1)],
                            yT[0:64, qlo:qlo + w], start=True, stop=not diag,
                            tile_position=(0, 0)), waits=sw, inc="sT")
                        emit("tensor", lambda e, j=j, qlo=qlo, w=w, pab=pab, diag=diag,
                             kTp=kTp: e.matmul(
                            pab[:, 512:512 + w], kTp[64:128, 128 * j:128 * (j + 1)],
                            yT[64:128, qlo:qlo + w], start=True, stop=not diag,
                            tile_position=(64, 0)), inc="sT")
                        if diag:
                            emit("tensor", lambda e, pab=pab: e.matmul(
                                pab[:, 0:128], cbf[:, C_I:C_I + 128],
                                cbf[:, C_R:C_R + 128], start=False, stop=True), inc="sT")
                            emit("tensor", lambda e, pab=pab: e.matmul(
                                pab[:, 512:640], cbf[:, C_I:C_I + 128],
                                cbf[:, C_R:C_R + 128], start=False, stop=True), inc="sT")
                        t_scores = ("sT", cnt["sT"])
                        slot = j % 2
                        ew = [t_scores, bias_tok_c[c]]
                        if slot in free_pt_tok:
                            ew.append(free_pt_tok[slot])
                        eA = emit("scalar", lambda e, slot=slot, w=w, pab=pab, c=c,
                                  bias_col=bias_col: e.activation(
                            pt[:, 1024 * slot:1024 * slot + 512 + w], pab[:, 0:512 + w],
                            AF.Exp, bias=biasMC[:, bias_col + c:bias_col + c + 1],
                            scale=0.125), waits=ew, inc="sS")
                        free_bank_tok[j % 2] = ("sS", eA[1])
                        exp_tok[j] = ("sS", eA[1])

                    def emit_lpv(j, c=c, bY=bY, bL=bL, jlast=jlast):
                        off = max(0, 128 * j - 512 * c)
                        w = 512 - off
                        pA = 1024 * (j % 2)
                        pB = 1024 * (j % 2) + 512
                        lw = [exp_tok[j], vproj_done]
                        if j == 0:
                            if (c % 2) in free_l:
                                lw.append(free_l[c % 2])
                            if (c % 2) in free_y:
                                lw.append(free_y[c % 2])
                            for bn in (4 + 2 * (c % 2), 5 + 2 * (c % 2)):
                                if bn in pbank_evac:
                                    lw.append(pbank_evac.pop(bn))
                        emit("tensor", lambda e, j=j, off=off, w=w, pA=pA, bL=bL:
                             e.matmul(
                            bL[0:1, off:512], cbf[:, C_ONE:C_ONE + 1],
                            pt[:, pA:pA + w],
                            start=(j == 0), stop=(j == jlast), tile_position=(0, 0)),
                            waits=lw, inc="sT")
                        emit("tensor", lambda e, j=j, off=off, w=w, pB=pB, bL=bL:
                             e.matmul(
                            bL[32:33, off:512], cbf[:, C_ONE:C_ONE + 1],
                            pt[:, pB:pB + w],
                            start=(j == 0), stop=(j == jlast), tile_position=(0, 32)),
                            inc="sT")
                        emit("tensor", lambda e, j=j, off=off, w=w, pA=pA, hA=hA,
                             bY=bY: e.matmul(
                            bY[0:64, off:512],
                            vtok[:, D * j + HD * hA: D * j + HD * hA + HD],
                            pt[:, pA:pA + w],
                            start=(j == 0), stop=(j == jlast), tile_position=(0, 0)),
                            inc="sT")
                        t_pv = emit("tensor", lambda e, j=j, off=off, w=w, pB=pB,
                                    hB=hB, bY=bY: e.matmul(
                            bY[64:128, off:512],
                            vtok[:, D * j + HD * hB: D * j + HD * hB + HD],
                            pt[:, pB:pB + w],
                            start=(j == 0), stop=(j == jlast), tile_position=(0, 64)),
                            inc="sT")
                        free_pt_tok[j % 2] = ("sT", t_pv[1])

                    for j in range(jlast + 2):
                        if j <= jlast:
                            emit_scores(j)
                        if pending_pre[0] is not None and j == 1:
                            pending_pre[0]()
                            pending_pre[0] = None
                        if pending_fin[0] is not None and j == (2 if c == 0 else 4):
                            pending_fin[0]()
                            pending_fin[0] = None
                        if ((c == 0 and j in (3, 4))
                                or (c == 1 and j in (5, 7))
                                or (c == 2 and j in (5, 7, 9, 11))
                                or (c == 3 and j in (5, 7, 9, 11, 13, 15, 16))):
                            run_bstep(6 if c % 2 == 0 else 4)
                        if j >= 1:
                            emit_lpv(j - 1)
                    att_last_pv = ("sT", cnt["sT"])
                    pv_last_c = att_last_pv

                    def fin_pre(c=c, bL=bL, pv_last_c=pv_last_c):
                        t_l1 = emit("vector", lambda e, bL=bL: e.tensor_copy(
                            rl[0:33, 0:512], bL[0:33, 0:512]), waits=[pv_last_c],
                            inc="sV")
                        emit("scalar", lambda e: e.activation(
                            rl[0:33, 0:512], rl[0:33, 0:512], AF.Ln,
                            bias=consts[0:33, 5:6], scale=1.0),
                            waits=[("sV", t_l1[1])], inc="sS")
                        t_rcp = emit("scalar", lambda e: e.activation(
                            rl[0:33, 0:512], rl[0:33, 0:512], AF.Exp,
                            bias=consts[0:33, 5:6], scale=-1.0), inc="sS", selfw=True)
                        fin_tok[0] = ("sS", t_rcp[1])
                        fin_tok[1] = ("sV", t_l1[1])

                    def fin(c=c, bY=bY, bL=bL, pv_last_c=pv_last_c, p=p):
                        bLn = 5 + 2 * (c % 2)
                        bYn = 4 + 2 * (c % 2)
                        rw = [fin_tok[0], fin_tok[1]]
                        if bLn in pbank_evac:
                            rw.append(pbank_evac[bLn])
                        emit("tensor", lambda e, bL=bL: e.matmul(
                            bL[0:64, 0:512], onesf[0:1, 0:64], rl[0:1, 0:512],
                            start=True, stop=True, tile_position=(0, 0)),
                            waits=rw, inc="sT")
                        t_rlb = emit("tensor", lambda e, bL=bL: e.matmul(
                            bL[64:128, 0:512], onesf[32:33, 0:64], rl[32:33, 0:512],
                            start=True, stop=True, tile_position=(32, 64)), inc="sT")
                        t_rlbe = emit("vector", lambda e, bL=bL: e.tensor_copy(
                            rlb[:, 0:512], bL[:, 0:512]), waits=[("sT", t_rlb[1])],
                            inc="sV")
                        t_ye = emit("vector", lambda e, p=p, c=c, bY=bY: e.tensor_mul(
                            yT[:, T * p + 512 * c: T * p + 512 * (c + 1)],
                            bY[:, 0:512], rlb[:, 0:512]),
                            waits=[pv_last_c], inc="sV", selfw=True)
                        pbank_evac[bLn] = ("sV", t_rlbe[1])
                        pbank_evac[bYn] = ("sV", t_ye[1])
                        free_l[c % 2] = ("sV", t_rlbe[1])
                        free_y[c % 2] = ("sV", t_ye[1])
                    pending_pre[0] = fin_pre
                    pending_fin[0] = fin
                while bq_i[0] < len(bsteps):
                    run_bstep(4)
                if 'tok_wq' in globals_:
                    tok_wq = globals_.pop('tok_wq')
                    tok_wk = globals_.pop('tok_wk')
                att_v_prev[0] = cnt["sV"]
                pair_exp_last[0] = ("sS", cnt["sS"])
            if pending_pre[0] is not None:
                pending_pre[0]()
                pending_pre[0] = None
            if pending_fin[0] is not None:
                pending_fin[0]()
                pending_fin[0] = None
            qkproj_last = qk_last[0]
            att_done_v = ("sV", cnt["sV"])
            att_done_t = ("sT", cnt["sT"])
            att_done_holder[0] = att_done_v
            att_sS_holder[0] = ("sS", cnt["sS"])
            tap("yt", lambda: yT[:, :], [128, NCH * T], BF16, [att_done_v])
            if STOP < 4:
                finish_debug()

            for i in range(NT):
                for h in range(2):
                    bnum = 6 + (2 * i + h) % 2
                    bank = PB[bnum]
                    bw = [att_done_v, ("dW", tok_wo[7][1])]
                    if bnum in pbank_evac:
                        bw.append(pbank_evac[bnum])
                    for a in range(NCH):
                        emit("tensor", lambda e, i=i, h=h, a=a, bank=bank: e.matmul(
                            bank[:, 0:512],
                            yT[:, T * a + 128 * i: T * a + 128 * (i + 1)],
                            wbuf[:, D * a + 512 * h: D * a + 512 * (h + 1)],
                            start=(a == 0), stop=False),
                            waits=bw if a == 0 else (), inc="sT")
                    t_mm = emit("tensor", lambda e, h=h, bank=bank: e.matmul(
                        bank[:, 0:512], cbf[32:33, C_ONE:C_ONE + 128],
                        brow3[32:33, 512 * h:512 * (h + 1)],
                        start=False, stop=True), inc="sT")
                    resid_tok[i] = emit("vector", lambda e, i=i, h=h, bank=bank:
                                        e.scalar_tensor_tensor(
                        xs[:, D * i + 512 * h: D * i + 512 * (h + 1)], bank[:, 0:512],
                        consts[:, 4:5], xs[:, D * i + 512 * h: D * i + 512 * (h + 1)],
                        op0=ALU.mult, op1=ALU.add), waits=[("sT", t_mm[1])], inc="sV")
                    pbank_evac[bnum] = ("sV", resid_tok[i][1])
            oproj_done_t = ("sT", cnt["sT"])
            oproj_v_holder[0] = ("sV", cnt["sV"])
            tap("xso", lambda: xs[:, :], [128, NT * D], F32,
                [("sV", resid_tok[NT - 1][1])])
            if STOP < 5:
                finish_debug()

            tok_wt = []
            for a in range(NCH):
                tok_wt.append(emit(
                    "gpsimd", lambda e, a=a: e.dma_start(
                        out=wbuf[:, D * a:D * (a + 1)], in_=wt_d[128 * a:128 * (a + 1), :]),
                    waits=[oproj_done_t], inc="dW", amt=16))
            tok_da = emit("gpsimd", lambda e: e.dma_start(out=qT[0:100, 0:1024], in_=da_d),
                          waits=[att_done_t, qkproj_last], inc="dW", amt=16)
            tok_upt = emit("gpsimd", lambda e: e.dma_start(out=qT[0:50, 1024:2048], in_=upt_d),
                           inc="dW", amt=16)
            tok_ldt = emit("gpsimd", lambda e: e.dma_start(out=pt[0:100, 0:4], in_=ldt_d),
                           inc="dW", amt=16)
            tok_lut = emit("gpsimd", lambda e: e.dma_start(out=pt[0:4, 4:54], in_=lut_d),
                           inc="dW", amt=16)

            tr2 = layernorm_round(1)
            ln2_done = [("sS", cnt["sS"]), ("sV", cnt["sV"])]
            tap("xht2", lambda: xhT[:, :], [128, NCH * T], BF16, list(ln2_done))

            for h in range(2):
                emit("tensor", lambda e, h=h: e.matmul(
                    pb6[0:4, 0:512], pt[0:100, 0:4], qT[0:100, 512 * h:512 * (h + 1)],
                    start=True, stop=True),
                    waits=[("dW", tok_lut[1]), ("sV", cnt["sV"])], inc="sT")
                emit("vector", lambda e, h=h: e.tensor_copy(
                    kT[0:4, 512 * h:512 * (h + 1)], pb6[0:4, 0:512]),
                    waits=[("sT", cnt["sT"])], inc="sV")
            for h in range(2):
                emit("tensor", lambda e, h=h: e.matmul(
                    pb6[0:50, 0:512], pt[0:4, 4:54], kT[0:4, 512 * h:512 * (h + 1)],
                    start=True, stop=True), waits=[("sV", cnt["sV"])], inc="sT")
                emit("vector", lambda e, h=h: e.tensor_copy(
                    kT[0:50, 1024 + 512 * h:1024 + 512 * (h + 1)], pb6[0:50, 0:512]),
                    waits=[("sT", cnt["sT"])], inc="sV")
            wefft_tok = [None] * NCH
            a2_tok = ("sV", cnt["sV"])
            for a in range(NCH):
                for h in range(2):
                    bank = PB[6 + (2 * a + h) % 2]
                    emit("tensor", lambda e, a=a, h=h, bank=bank: e.matmul(
                        bank[:, 0:512], kT[0:50, 1024 + 128 * a:1024 + 128 * (a + 1)],
                        qT[0:50, 1024 + 512 * h:1024 + 512 * (h + 1)],
                        start=True, stop=True),
                        waits=[a2_tok, ("sV", cnt["sV"])], inc="sT")
                    wefft_tok[a] = emit("vector", lambda e, a=a, h=h, bank=bank: e.tensor_add(
                        yT[:, D * a + 512 * h: D * a + 512 * (h + 1)], bank[:, 0:512],
                        wbuf[:, D * a + 512 * h: D * a + 512 * (h + 1)]),
                        waits=[("sT", cnt["sT"]), ("dW", tok_lut[1]), att_done_v],
                        inc="sV")

            tap("weff", lambda: yT[:, 0:NCH * D], [128, NCH * D], BF16,
                [("sV", wefft_tok[7][1])])

            ffn_resid = [None] * NT
            for i in range(NT):
                for h in range(2):
                    bnum = 6 + (2 * i + h) % 2
                    bank = PB[bnum]
                    bw = list(tr2[i]) + [("sV", wefft_tok[7][1])]
                    if bnum in pbank_evac:
                        bw.append(pbank_evac[bnum])
                    for a in range(NCH):
                        emit("tensor", lambda e, i=i, h=h, a=a, bank=bank: e.matmul(
                            bank[:, 0:512],
                            xhT[:, T * a + 128 * i: T * a + 128 * (i + 1)],
                            yT[:, D * a + 512 * h: D * a + 512 * (h + 1)],
                            start=(a == 0), stop=False),
                            waits=bw if a == 0 else (), inc="sT")
                    t_mm = emit("tensor", lambda e, h=h, bank=bank: e.matmul(
                        bank[:, 0:512], cbf[64:65, C_ONE:C_ONE + 128],
                        brow3[64:65, 512 * h:512 * (h + 1)],
                        start=False, stop=True), inc="sT")
                    ffn_resid[i] = emit("vector", lambda e, i=i, h=h, bank=bank:
                                        e.scalar_tensor_tensor(
                        xs[:, D * i + 512 * h: D * i + 512 * (h + 1)], bank[:, 0:512],
                        consts[:, 4:5], xs[:, D * i + 512 * h: D * i + 512 * (h + 1)],
                        op0=ALU.mult, op1=ALU.add), waits=[("sT", t_mm[1])], inc="sV")
                    pbank_evac[bnum] = ("sV", ffn_resid[i][1])

            for g in range(4):
                dst = y_d[512 * g:512 * (g + 1), :].rearrange("(i p) c -> p i c", p=128)
                src = xs[:, 4096 * g:4096 * (g + 1)].rearrange("p (i c) -> p i c", i=4)
                ws = [("sV", ffn_resid[4 * g + k][1]) for k in range(4)]
                emit("sync", lambda e, src=src, dst=dst: e.dma_start(out=dst, in_=src),
                     waits=ws, inc="dOut", amt=16)
            emit("sync", lambda e, v=cnt["dOut"] + 64: None, waits=[("dOut", cnt["dOut"])])

        except _Done:
            pass
        def replay(eng_obj, name):
            for ws, fn, inc, amt in plan[name]:
                for s, v in ws:
                    eng_obj.wait_ge(sems[s], v)
                ins = fn(eng_obj)
                if inc is not None:
                    ins.then_inc(sems[inc], amt)

        @block.tensor
        def _(t):
            replay(t, "tensor")

        @block.vector
        def _(v):
            replay(v, "vector")

        @block.scalar
        def _(s):
            replay(s, "scalar")

        @block.gpsimd
        def _(g):
            replay(g, "gpsimd")

        @block.sync
        def _(y):
            replay(y, "sync")

    return nc


def kernel(L0_attn_x, L0_attn_Wq, L0_attn_bq, L0_attn_Wk, L0_attn_bk,
           L0_attn_Wv, L0_attn_bv, L0_attn_Wo, L0_attn_bo,
           L0_ffn_W, L0_ffn_b, L0_ffn_lora_down, L0_ffn_lora_up,
           L0_ffn_down_aux, L0_ffn_up_aux):
    x = np.ascontiguousarray(np.asarray(L0_attn_x, np.float32))
    Wq = np.asarray(L0_attn_Wq, np.float32)
    Wk = np.asarray(L0_attn_Wk, np.float32)

    def dev_layout(W):
        return np.ascontiguousarray(
            W.reshape(8, 128, 8, 128).transpose(2, 1, 0, 3).reshape(1024, 1024))

    wq_dev = dev_layout(Wq)
    wk_dev = dev_layout(Wk)
    wv = np.ascontiguousarray(np.asarray(L0_attn_Wv, np.float32))
    wo = np.ascontiguousarray(np.asarray(L0_attn_Wo, np.float32))
    wt = np.ascontiguousarray(np.asarray(L0_ffn_W, np.float32).T)
    bq = np.asarray(L0_attn_bq, np.float32)
    bk = np.asarray(L0_attn_bk, np.float32)
    bqk = np.ascontiguousarray(np.concatenate(
        [bq.reshape(8, 128).T, bk.reshape(8, 128).T], axis=1))
    brow = np.ascontiguousarray(np.stack(
        [np.asarray(L0_attn_bv, np.float32), np.asarray(L0_attn_bo, np.float32),
         np.asarray(L0_ffn_b, np.float32)], axis=0))
    consts = np.zeros((128, 8), np.float32)
    consts[:, 0] = 1.0 / 1024
    consts[:, 1] = -1.0 / 1024
    consts[:, 2] = 1e-5
    consts[:, 3] = -0.125
    consts[:, 4] = 1.0
    consts[:, 5] = 0.0
    consts[:, 6] = -1.0
    idx = np.arange(128)
    cbf = np.zeros((128, 386), np.float32)
    cbf[idx, idx] = 1.0
    cbf[:, 128:256] = np.where(idx[:, None] > idx[None, :], -30000.0, 0.0)
    cbf[:, 256:384] = 1.0
    cbf[idx < 64, 384] = 1.0
    cbf[idx >= 64, 385] = 1.0
    cbf = cbf.astype(bfdt)
    ldt = np.ascontiguousarray(np.asarray(L0_ffn_lora_down, np.float32).T)
    lut = np.ascontiguousarray(np.asarray(L0_ffn_lora_up, np.float32).T)
    upt = np.ascontiguousarray(np.asarray(L0_ffn_up_aux, np.float32).T)
    da = np.ascontiguousarray(np.asarray(L0_ffn_down_aux, np.float32))

    shared = {"wq": wq_dev, "wk": wk_dev, "wv": wv, "wo": wo, "wt": wt,
              "bqk": bqk, "brow": brow, "consts": consts, "cbf": cbf,
              "ldt": ldt, "lut": lut, "upt": upt, "da": da}
    in_maps = [dict(shared, x=x[core]) for core in range(8)]

    nc = build_nc()
    res = run_bass_kernel_spmd(nc, in_maps, core_ids=list(range(8)))
    out = np.stack([np.asarray(res.results[i]["y"]) for i in range(8)], axis=0)
    return out.astype(np.float32)


# version 33
# speedup vs baseline: 1.0644x; 1.0064x over previous
import numpy as np
import ml_dtypes
import concourse.bass as bass
import concourse.mybir as mybir
from concourse.bass_utils import run_bass_kernel_spmd

F32 = mybir.dt.float32
BF16 = mybir.dt.bfloat16
AF = mybir.ActivationFunctionType
ALU = mybir.AluOpType
AX = mybir.AxisListType
bfdt = ml_dtypes.bfloat16

T = 2048
D = 1024
NT = 16
NCH = 8
NPAIR = 8
HD = 64


def build_nc(debug=(), stop="full", ln_tiles=NT):
    STOP = {"ln1": 1, "vproj": 2, "pairs": 3, "oproj": 4, "ln2": 5,
            "weff": 6, "full": 9}[stop]
    nc = bass.Bass("TRN2", target_bir_lowering=False)

    x_d = nc.dram_tensor("x", [T, D], F32, kind="ExternalInput").ap()
    wq_d = nc.dram_tensor("wq", [D, D], F32, kind="ExternalInput").ap()
    wk_d = nc.dram_tensor("wk", [D, D], F32, kind="ExternalInput").ap()
    wv_d = nc.dram_tensor("wv", [D, D], F32, kind="ExternalInput").ap()
    wo_d = nc.dram_tensor("wo", [D, D], F32, kind="ExternalInput").ap()
    wt_d = nc.dram_tensor("wt", [D, D], F32, kind="ExternalInput").ap()
    bqk_d = nc.dram_tensor("bqk", [128, 16], F32, kind="ExternalInput").ap()
    brow_d = nc.dram_tensor("brow", [3, D], F32, kind="ExternalInput").ap()
    consts_d = nc.dram_tensor("consts", [128, 8], F32, kind="ExternalInput").ap()
    cbf_d = nc.dram_tensor("cbf", [128, 386], BF16, kind="ExternalInput").ap()
    ldt_d = nc.dram_tensor("ldt", [100, 4], F32, kind="ExternalInput").ap()
    lut_d = nc.dram_tensor("lut", [4, 50], F32, kind="ExternalInput").ap()
    upt_d = nc.dram_tensor("upt", [50, D], F32, kind="ExternalInput").ap()
    da_d = nc.dram_tensor("da", [100, D], F32, kind="ExternalInput").ap()
    y_d = nc.dram_tensor("y", [T, D], F32, kind="ExternalOutput").ap()

    import contextlib
    with contextlib.ExitStack() as _st:
        xs = _st.enter_context(nc.sbuf_tensor("xs", [128, NT * D], F32))
        xhT = _st.enter_context(nc.sbuf_tensor("xhT", [128, NCH * T], BF16))
        vtok = _st.enter_context(nc.sbuf_tensor("vtok", [128, NT * D], BF16))
        yT = _st.enter_context(nc.sbuf_tensor("yT", [128, NCH * T], BF16))
        qT = _st.enter_context(nc.sbuf_tensor("qT", [128, T], BF16))
        kT = _st.enter_context(nc.sbuf_tensor("kT", [128, T], BF16))
        pt = _st.enter_context(nc.sbuf_tensor("pt", [128, 4 * 512], BF16))
        wqs = _st.enter_context(nc.sbuf_tensor("wqs", [128, D], BF16))
        wks = _st.enter_context(nc.sbuf_tensor("wks", [128, D], BF16))
        wbuf = _st.enter_context(nc.sbuf_tensor("wbuf", [128, NCH * D], BF16))
        sqspan = _st.enter_context(nc.sbuf_tensor("sqspan", [128, 512], BF16))
        sqspan2 = _st.enter_context(nc.sbuf_tensor("sqspan2", [128, 512], BF16))
        brow3 = _st.enter_context(nc.sbuf_tensor("brow3", [65, D], BF16))
        rl = _st.enter_context(nc.sbuf_tensor("rl", [33, 512], F32))
        rlb = _st.enter_context(nc.sbuf_tensor("rlb", [128, 512], F32))
        stat = _st.enter_context(nc.sbuf_tensor("stat", [128, 128], F32))
        bqk = _st.enter_context(nc.sbuf_tensor("bqk_s", [128, 16], F32))
        consts = _st.enter_context(nc.sbuf_tensor("consts_s", [128, 8], F32))
        cbf = _st.enter_context(nc.sbuf_tensor("cbf_s", [128, 386], BF16))
        onesf = _st.enter_context(nc.sbuf_tensor("onesf", [33, 128], F32))
        qnrow = _st.enter_context(nc.sbuf_tensor("qnrow", [1, 64], F32))
        biasMC = _st.enter_context(nc.sbuf_tensor("biasMC", [128, 16], F32))
        pAB0 = _st.enter_context(nc.psum_tensor("pAB0", [128, 1024], F32))
        pAB1 = _st.enter_context(nc.psum_tensor("pAB1", [128, 1024], F32))
        pb4 = _st.enter_context(nc.psum_tensor("pb4", [128, 512], F32))
        pb5 = _st.enter_context(nc.psum_tensor("pb5", [128, 512], F32))
        pb6 = _st.enter_context(nc.psum_tensor("pb6", [128, 512], F32))
        pb7 = _st.enter_context(nc.psum_tensor("pb7", [128, 512], F32))
        sT = _st.enter_context(nc.semaphore("sT"))
        sS = _st.enter_context(nc.semaphore("sS"))
        sV = _st.enter_context(nc.semaphore("sV"))
        dW = _st.enter_context(nc.semaphore("dW"))
        dIn = _st.enter_context(nc.semaphore("dIn"))
        dSc = _st.enter_context(nc.semaphore("dSc"))
        dOut = _st.enter_context(nc.semaphore("dOut"))
        dX0 = _st.enter_context(nc.semaphore("dX0"))
        dX1 = _st.enter_context(nc.semaphore("dX1"))
        dX2 = _st.enter_context(nc.semaphore("dX2"))
        dX3 = _st.enter_context(nc.semaphore("dX3"))
        block = _st.enter_context(nc.Block())

        sems = {"sT": sT, "sS": sS, "sV": sV, "dW": dW, "dIn": dIn,
                "dSc": dSc, "dOut": dOut, "dX0": dX0, "dX1": dX1,
                "dX2": dX2, "dX3": dX3}
        own = {"tensor": "sT", "scalar": "sS", "vector": "sV"}
        plan = {e: [] for e in ("tensor", "vector", "scalar", "gpsimd", "sync")}
        cnt = {s: 0 for s in sems}
        lastw = {e: {s: 0 for s in sems} for e in plan}

        def emit(eng, fn, waits=(), inc=None, amt=1, selfw=False):
            waits = list(waits)
            if selfw and eng in own:
                waits.append((own[eng], cnt[own[eng]]))
            ws = []
            for s, v in waits:
                if v <= 0:
                    continue
                if own.get(eng) == s and not selfw:
                    continue
                if lastw[eng][s] >= v:
                    continue
                ws.append((s, v))
                lastw[eng][s] = v
            tok = None
            if inc is not None:
                cnt[inc] += amt
                tok = (inc, cnt[inc])
            plan[eng].append((ws, fn, inc, amt))
            return tok

        PB = [None, None, None, None, pb4, pb5, pb6, pb7]
        PAB = [pAB0, pAB1]
        C_I, C_R, C_ONE, C_E2 = 0, 128, 256, 384

        def tap(name, ap_fn, shape, dtype, waits):
            if name not in debug:
                return
            t = nc.dram_tensor("dbg_" + name, shape, dtype, kind="ExternalOutput").ap()
            emit("sync", lambda e, t=t: e.dma_start(out=t, in_=ap_fn()),
                 waits=waits, inc="dOut", amt=16)

        try:
            for wi in range(24):
                emit("tensor", lambda e, wi=wi: e.matmul(
                    PB[6 + wi % 2][:, 0:512], wbuf[0:128, 0:128],
                    wbuf[:, 1024:1536], start=True, stop=True), inc="sT")

            tok_consts = emit("sync", lambda e: e.dma_start(out=consts[:, :], in_=consts_d),
                              inc="dIn", amt=16)
            tok_cbf = emit("sync", lambda e: e.dma_start(out=cbf[:, :], in_=cbf_d),
                           inc="dIn", amt=16)
            tok_bqk = emit("sync", lambda e: e.dma_start(out=bqk[:, :], in_=bqk_d),
                           inc="dIn", amt=16)
            tok_onesf = emit("vector", lambda e: e.memset(onesf[:, :], 1.0), inc="sV")
            tok_x = []
            for g in range(4):
                for t4 in range(4):
                    i = 4 * g + t4
                    xw = [("dX%d" % (g - 1), 64)] if (t4 == 0 and g > 0) else ()
                    tok_x.append(emit(
                        "sync", lambda e, i=i: e.dma_start(
                            out=xs[:, 1024 * i:1024 * (i + 1)],
                            in_=x_d[128 * i:128 * (i + 1), :]),
                        waits=xw, inc="dX%d" % g, amt=16))

            tok_wv = []
            for a in range(NCH):
                tok_wv.append(emit(
                    "gpsimd", lambda e, a=a: e.dma_start(
                        out=wbuf[:, D * a:D * (a + 1)], in_=wv_d[128 * a:128 * (a + 1), :]),
                    waits=[("dX0", 64)] if a == 0 else (), inc="dW", amt=16))
            for r in range(3):
                tok_brow = emit("gpsimd", lambda e, r=r: e.dma_start(
                    out=brow3[32 * r:32 * r + 1, :], in_=brow_d[r:r + 1, :]),
                    inc="dW", amt=16)
            tok_wk = emit("gpsimd", lambda e: e.dma_start(out=wks[:, :], in_=wk_d[0:128, :]),
                          waits=[("dW", tok_wv[-1][1])], inc="dW", amt=16)
            tok_wq = emit("gpsimd", lambda e: e.dma_start(out=wqs[:, :], in_=wq_d[0:128, :]),
                          inc="dW", amt=16)

            class _Done(Exception):
                pass

            def finish_debug():
                emit("sync", lambda e: None, waits=[("dOut", cnt["dOut"])])
                raise _Done()

            def layernorm_round(rnd, after_tile=None):
                tr_tok = [None] * NT
                nt_r = ln_tiles if rnd == 0 else NT
                xc_tok = [None] * NT
                dg_tok = [None] * NT
                slot_free = {}
                tbank_read = {}
                t_inv = {}

                def sc(i):
                    return 16 * (i % 8)

                groups = [list(range(4 * g, min(4 * g + 4, nt_r)))
                          for g in range((nt_r + 3) // 4)]

                t_nmd = {}

                def stageA_V(g):
                    tiles = groups[g]
                    for i in tiles:
                        if rnd == 0:
                            wl = [("dX%d" % g, 16 * (i % 4 + 1)),
                                  ("dIn", tok_bqk[1])]
                        else:
                            wl = [("sV", resid_tok[i][1])]
                        emit("vector", lambda e, i=i: e.bn_stats(
                            stat[:, sc(i):sc(i) + 6], xs[:, D * i:D * i + 512]),
                            waits=wl + [("dIn", tok_consts[1])], inc="sV")
                        emit("vector", lambda e, i=i: e.bn_stats(
                            stat[:, sc(i) + 6:sc(i) + 12],
                            xs[:, D * i + 512:D * i + 1024]), inc="sV")
                    for i in tiles:
                        emit("vector", lambda e, i=i: e.bn_aggr(
                            stat[:, sc(i) + 12:sc(i) + 14], stat[:, sc(i):sc(i) + 12]),
                            inc="sV", selfw=True)
                    for i in tiles:
                        emit("vector", lambda e, i=i: e.tensor_scalar(
                            stat[:, sc(i) + 3:sc(i) + 4], stat[:, sc(i) + 12:sc(i) + 13],
                            consts[:, 6:7], None, op0=ALU.mult), inc="sV", selfw=True)
                    t_nmd[g] = ("sV", cnt["sV"])

                def stageA_S(g):
                    tiles = groups[g]
                    for i in tiles:
                        emit("scalar", lambda e, i=i: e.activation(
                            stat[:, sc(i) + 9:sc(i) + 10], stat[:, sc(i) + 13:sc(i) + 14],
                            AF.Ln, bias=consts[:, 2:3], scale=1.0),
                            waits=[t_nmd[g]], inc="sS")
                    for i in tiles:
                        t_inv[g] = emit("scalar", lambda e, i=i: e.activation(
                            stat[:, sc(i) + 10:sc(i) + 11], stat[:, sc(i) + 9:sc(i) + 10],
                            AF.Exp, bias=consts[:, 5:6], scale=-0.5),
                            inc="sS", selfw=True)
                    for i in tiles:
                        nw = [("sS", t_inv[g][1])]
                        if rnd == 1:
                            nw.append(("sT", att_last_pv[1]))
                        xc_tok[i] = emit("scalar", lambda e, i=i: e.add(
                            vtok[:, 1024 * i:1024 * i + D], xs[:, D * i:D * (i + 1)],
                            stat[:, sc(i) + 3:sc(i) + 4]),
                            waits=nw, inc="sS", selfw=True)

                pending_at = [None]

                def flush_at():
                    if pending_at[0] is not None and after_tile is not None:
                        after_tile(*pending_at[0])
                    pending_at[0] = None

                def stageB(g):
                    tiles = groups[g]
                    tg = t_inv[g]
                    for i in tiles:
                        dw = [("dIn", tok_cbf[1]), ("sS", tg[1])]
                        if (i % 4) in slot_free:
                            dw.append(slot_free[i % 4])
                        dg_tok[i] = emit("vector", lambda e, i=i: e.tensor_scalar(
                            sqspan[:, 128 * (i % 4):128 * (i % 4) + 128],
                            cbf[:, C_I:C_I + 128], stat[:, sc(i) + 10:sc(i) + 11],
                            None, op0=ALU.mult), waits=dw, inc="sV", selfw=True)
                    if rnd == 0 and 0 in tiles:
                        tap("stat0", lambda: stat[:, :], [128, 128], F32,
                            [("sV", dg_tok[0][1])])
                    for i in tiles:
                        flush_at()
                        mw0 = [xc_tok[i], ("sV", dg_tok[i][1])]
                        if rnd == 1:
                            mw0.append(att_sS_holder[0])
                        for a in range(NCH):
                            bnum = a % 4
                            bank = PAB[bnum // 2]
                            col = 512 * (bnum % 2) + 128 * (a // 4)
                            bw2 = list(mw0) if a == 0 else []
                            if bnum in tbank_read:
                                bw2.append(tbank_read[bnum])
                            emit("tensor", lambda e, i=i, a=a, bank=bank, col=col:
                                 e.matmul(
                                bank[:, col:col + 128],
                                vtok[:, 1024 * i + 128 * a:1024 * i + 128 * (a + 1)],
                                sqspan[:, 128 * (i % 4):128 * (i % 4) + 128],
                                start=True, stop=True), waits=bw2, inc="sT")
                            ew = [("sT", cnt["sT"])]
                            if rnd == 1:
                                ew.append(("sT", qkproj_last[1]))
                            if a < 2:
                                tr_tok[i] = emit("vector", lambda e, i=i, a=a, bank=bank,
                                                 col=col: e.tensor_copy(
                                    xhT[:, T * a + 128 * i: T * a + 128 * (i + 1)],
                                    bank[:, col:col + 128]), waits=ew, inc="sV")
                                tbank_read[bnum] = ("sV", tr_tok[i][1])
                            else:
                                tr_tok[i] = emit("scalar", lambda e, i=i, a=a, bank=bank,
                                                 col=col: e.copy(
                                    xhT[:, T * a + 128 * i: T * a + 128 * (i + 1)],
                                    bank[:, col:col + 128]), waits=ew, inc="sS")
                                tbank_read[bnum] = ("sS", tr_tok[i][1])
                        slot_free[i % 4] = ("sT", cnt["sT"])
                        tr_tok[i] = [("sS", cnt["sS"]), ("sV", cnt["sV"])]
                        pending_at[0] = (i, tr_tok[i])

                if groups:
                    stageA_V(0)
                    stageA_S(0)
                    for g in range(1, len(groups)):
                        stageA_V(g)
                        stageB(g - 1)
                        stageA_S(g)
                    stageB(len(groups) - 1)
                    flush_at()
                return tr_tok

            resid_tok = [None] * NT
            att_done_holder = [None]
            oproj_v_holder = [None]
            att_sS_holder = [None]
            att_last_pv = None
            qkproj_last = None

            vproj_tok = [None] * NT
            pbank_evac = {}
            CB = [6]
            ln_gate = [[]]
            chain_guard = [None]
            ln1_evac = [None]

            def vproj_tile(i, trtok):
                for h in range(2):
                    bnum = 6 + (2 * i + h) % 2
                    bank = PB[bnum]
                    bw = list(trtok) + [("dW", tok_wk[1]), ("dIn", tok_cbf[1])]
                    if bnum in pbank_evac:
                        bw.append(pbank_evac[bnum])
                    for a in range(NCH):
                        emit("tensor", lambda e, i=i, h=h, a=a, bank=bank: e.matmul(
                            bank[:, 0:512],
                            xhT[:, T * a + 128 * i: T * a + 128 * (i + 1)],
                            wbuf[:, D * a + 512 * h: D * a + 512 * (h + 1)],
                            start=(a == 0), stop=False),
                            waits=bw if a == 0 else (), inc="sT")
                    t_mm = emit("tensor", lambda e, h=h, bank=bank: e.matmul(
                        bank[:, 0:512], cbf[0:1, C_ONE:C_ONE + 128],
                        brow3[0:1, 512 * h:512 * (h + 1)],
                        start=False, stop=True), inc="sT")
                    vproj_tok[i] = emit("vector", lambda e, i=i, h=h, bank=bank:
                                        e.tensor_copy(
                        vtok[:, D * i + 512 * h: D * i + 512 * (h + 1)], bank[:, 0:512]),
                        waits=[("sT", t_mm[1])], inc="sV")
                    pbank_evac[bnum] = ("sV", vproj_tok[i][1])

            att_v_prev = [cnt["sV"]]
            pair_exp_last = [("sS", 0)]
            globals_ = {}
            KB = [kT, qT]
            pending_fin = [None]
            pending_pre = [None]
            fin_tok = [None, None]
            free_y = {}
            free_l = {}
            state = {}

            def emit_group(tp, tgt, stage, s, bcol, evmap):
                bnum = CB[0] + s % 2
                bank = PB[bnum]
                st = state[tp]
                bw = [st["wq"], st["wk"], ("dIn", tok_bqk[1]),
                      ("sV", att_v_prev[0])] + list(ln_gate[0])
                if bnum in pbank_evac:
                    bw.append(pbank_evac[bnum])
                for a in range(NCH):
                    emit("tensor", lambda e, s=s, a=a, bank=bank, stage=stage: e.matmul(
                        bank[:, 0:512],
                        stage[:, 128 * a:128 * (a + 1)],
                        xhT[:, T * a + 512 * s: T * a + 512 * (s + 1)],
                        start=(a == 0), stop=(a == NCH - 1)),
                        waits=bw if a == 0 else (), inc="sT")
                t_mm = ("sT", cnt["sT"])
                ev = emit("vector", lambda e, s=s, bank=bank, tgt=tgt, bcol=bcol:
                          e.tensor_scalar(tgt[:, 512 * s:512 * (s + 1)],
                                          bank[:, 0:512], bqk[:, bcol:bcol + 1],
                                          None, op0=ALU.add),
                          waits=[t_mm], inc="sV")
                pbank_evac[bnum] = ("sV", ev[1])
                evmap[s] = ("sV", ev[1])
                if tp == NPAIR - 1 and tgt is not kT and tgt is not qT and s == 3:
                    pass

            def emit_chain(tp, tgt, s, base, par):
                st = state[tp]
                sqb = sqspan[:, 0:512] if par == 0 else sqspan2[:, 0:512]
                evtok = (st["qevac"] if base == 0 else st["kevac"])[s]
                qb = 32 * (tp % 2)
                vw = [evtok]
                if chain_guard[0] is not None:
                    vw.append(chain_guard[0])
                if par in st["sq_prev"]:
                    vw.append(st["sq_prev"][par])
                t_sq = emit("vector", lambda e, tgt=tgt, s=s, sqb=sqb: e.tensor_mul(
                    sqb, tgt[:, 512 * s:512 * (s + 1)],
                    tgt[:, 512 * s:512 * (s + 1)]), waits=vw, inc="sV")
                for hh in range(2):
                    bnum = CB[0] + (s + hh) % 2
                    qbank = PB[bnum]
                    mw = [("sV", t_sq[1])]
                    if bnum in pbank_evac:
                        mw.append(pbank_evac[bnum])
                    t_mm = emit("tensor", lambda e, hh=hh, qbank=qbank, sqb=sqb:
                                e.matmul(
                        qbank[0:1, 0:512],
                        cbf[:, C_E2 + hh:C_E2 + hh + 1], sqb,
                        start=True, stop=True), waits=mw, inc="sT")
                    emit("vector", lambda e, base=base, hh=hh, s=s, qbank=qbank, qb=qb:
                         e.tensor_reduce(
                        qnrow[0:1, qb + base + 4 * hh + s: qb + base + 4 * hh + s + 1],
                        qbank[0:1, 0:512], axis=AX.X, op=ALU.max),
                        waits=[("sT", t_mm[1])], inc="sV")
                    pbank_evac[bnum] = ("sV", cnt["sV"])
                st["sq_prev"][par] = ("sT", cnt["sT"])

            def emit_knm(tp):
                qb = 32 * (tp % 2)
                for hh in range(2):
                    emit("vector", lambda e, hh=hh, qb=qb: e.tensor_reduce(
                        qnrow[0:1, qb + 16 + hh:qb + 17 + hh],
                        qnrow[0:1, qb + 8 + 4 * hh:qb + 12 + 4 * hh],
                        axis=AX.X, op=ALU.max), inc="sV", selfw=True)

            def emit_nm(tp, c):
                qb = 32 * (tp % 2)
                for hh in range(2):
                    emit("vector", lambda e, c=c, hh=hh, qb=qb: e.tensor_scalar(
                        qnrow[0:1, qb + 18 + 4 * hh + c:qb + 19 + 4 * hh + c],
                        qnrow[0:1, qb + 4 * hh + c:qb + 4 * hh + c + 1],
                        qnrow[0:1, qb + 16 + hh:qb + 17 + hh], None, op0=ALU.mult),
                        inc="sV", selfw=True)
                emit("vector", lambda e, c=c, qb=qb: e.tensor_max(
                    qnrow[0:1, qb + 26 + c:qb + 27 + c],
                    qnrow[0:1, qb + 18 + c:qb + 19 + c],
                    qnrow[0:1, qb + 22 + c:qb + 23 + c]), inc="sV", selfw=True)
                emit("scalar", lambda e, c=c, qb=qb: e.activation(
                    qnrow[0:1, qb + 26 + c:qb + 27 + c],
                    qnrow[0:1, qb + 26 + c:qb + 27 + c], AF.Ln,
                    bias=consts[0:1, 2:3], scale=1.0),
                    waits=[("sV", cnt["sV"])], inc="sS")
                t_s2 = emit("scalar", lambda e, c=c, qb=qb: e.activation(
                    qnrow[0:1, qb + 26 + c:qb + 27 + c],
                    qnrow[0:1, qb + 26 + c:qb + 27 + c], AF.Exp,
                    bias=consts[0:1, 5:6], scale=0.5), inc="sS", selfw=True)
                state[tp]["nm"][c] = emit("vector", lambda e, c=c, qb=qb: e.tensor_scalar(
                    qnrow[0:1, qb + 26 + c:qb + 27 + c],
                    qnrow[0:1, qb + 26 + c:qb + 27 + c],
                    consts[0:1, 3:4], None, op0=ALU.mult),
                    waits=[("sS", t_s2[1])], inc="sV", selfw=True)

            def emit_bcast(tp, c, boff):
                st = state[tp]
                qb = 32 * (tp % 2)
                bnum = CB[0] + boff
                bank = PB[bnum]
                bw = [st["nm"][c], ("sV", tok_onesf[1]), pair_exp_last[0]]
                if bnum in pbank_evac:
                    bw.append(pbank_evac[bnum])
                t_bc = emit("tensor", lambda e, c=c, bank=bank, qb=qb: e.matmul(
                    bank[:, 0:1], onesf[0:1, 0:128],
                    qnrow[0:1, qb + 26 + c:qb + 27 + c], start=True, stop=True),
                    waits=bw, inc="sT")
                e2 = emit("vector", lambda e, c=c, bank=bank, tp=tp: e.tensor_copy(
                    biasMC[:, 8 * (tp % 2) + c:8 * (tp % 2) + c + 1], bank[:, 0:1]),
                    waits=[("sT", t_bc[1])], inc="sV")
                pbank_evac[bnum] = ("sV", e2[1])
                st["bias"][c] = ("sV", e2[1])

            def new_state(tp):
                state[tp] = {
                    "wq": tok_wq, "wk": tok_wk, "kevac": {}, "qevac": {},
                    "nm": {}, "bias": {}, "sq_prev": {},
                    "kT": KB[tp % 2],
                    "qv": T * tp,
                }

            def boundary_steps(tp, with_groups=True):
                st = state[tp]
                ktgt = st["kT"]
                qtgt_base = st["qv"]
                qview = yT[:, qtgt_base:qtgt_base + T]
                steps = []
                if not with_groups:
                    steps.append(lambda: emit_chain(tp, ktgt, 0, 8, 0))
                    steps.append(lambda: emit_chain(tp, ktgt, 1, 8, 1))
                    steps.append(lambda: emit_chain(tp, ktgt, 2, 8, 0))
                    steps.append(lambda: (emit_chain(tp, ktgt, 3, 8, 1),
                                          emit_knm(tp)))
                    steps.append(lambda: (emit_chain(tp, qview, 0, 0, 0),
                                          emit_nm(tp, 0)))
                    steps.append(lambda: (emit_chain(tp, qview, 1, 0, 1),
                                          emit_nm(tp, 1)))
                    steps.append(lambda: (emit_chain(tp, qview, 2, 0, 0),
                                          emit_nm(tp, 2), emit_bcast(tp, 0, 0)))
                    steps.append(lambda: (emit_chain(tp, qview, 3, 0, 1),
                                          emit_nm(tp, 3), emit_bcast(tp, 1, 1),
                                          prefetch(tp)))
                    steps.append(lambda: emit_bcast(tp, 2, 0))
                    steps.append(lambda: emit_bcast(tp, 3, 1))
                    return steps
                steps.append(lambda: emit_group(tp, ktgt, wks, 0, 8 + tp, st["kevac"]))
                steps.append(lambda: (emit_group(tp, ktgt, wks, 1, 8 + tp, st["kevac"]),
                                      emit_chain(tp, ktgt, 0, 8, 0)))
                steps.append(lambda: (emit_group(tp, ktgt, wks, 2, 8 + tp, st["kevac"]),
                                      emit_chain(tp, ktgt, 1, 8, 1)))
                steps.append(lambda: (emit_group(tp, ktgt, wks, 3, 8 + tp, st["kevac"]),
                                      emit_chain(tp, ktgt, 2, 8, 0)))
                steps.append(lambda: (emit_chain(tp, ktgt, 3, 8, 1), emit_knm(tp)))
                steps.append(lambda: emit_group(tp, qview, wqs, 0, tp, st["qevac"]))
                steps.append(lambda: (emit_chain(tp, qview, 0, 0, 0), emit_nm(tp, 0)))
                steps.append(lambda: emit_group(tp, qview, wqs, 1, tp, st["qevac"]))
                steps.append(lambda: (emit_chain(tp, qview, 1, 0, 1), emit_nm(tp, 1)))
                steps.append(lambda: (emit_group(tp, qview, wqs, 2, tp, st["qevac"]),
                                      emit_bcast(tp, 0, 0)))
                steps.append(lambda: (emit_chain(tp, qview, 2, 0, 0), emit_nm(tp, 2)))
                steps.append(lambda: (emit_group(tp, qview, wqs, 3, tp, st["qevac"]),
                                      emit_bcast(tp, 1, 1)))
                steps.append(lambda: (emit_chain(tp, qview, 3, 0, 1), emit_nm(tp, 3),
                                      prefetch(tp)))
                steps.append(lambda: emit_bcast(tp, 2, 0))
                steps.append(lambda: emit_bcast(tp, 3, 1))
                if tp == NPAIR - 1:
                    steps.append(lambda: qk_last_mark())
                return steps

            def prefetch(tp):
                if tp < NPAIR - 1:
                    globals_['tok_wq'] = emit(
                        "gpsimd", lambda e, tp=tp: e.dma_start(
                            out=wqs[:, :],
                            in_=wq_d[128 * (tp + 1):128 * (tp + 2), :]),
                        waits=[("sT", cnt["sT"])], inc="dSc", amt=16)
                    globals_['tok_wk'] = emit(
                        "gpsimd", lambda e, tp=tp: e.dma_start(
                            out=wks[:, :],
                            in_=wk_d[128 * (tp + 1):128 * (tp + 2), :]),
                        inc="dSc", amt=16)

            qk_last = [None]

            def qk_last_mark():
                qk_last[0] = ("sT", cnt["sT"])


            new_state(0)
            p0_sched = {3: [("k", 0)], 5: [("q", 0)], 7: [("k", 1)],
                        9: [("q", 1)], 11: [("k", 2)], 13: [("q", 2)],
                        15: [("k", 3), ("q", 3)]}

            def ln1_cb(i, trtok):
                vproj_tile(i, trtok)
                ln_gate[0] = list(trtok)
                st0 = state[0]
                for kind, s in p0_sched.get(i, ()):
                    if kind == "k":
                        emit_group(0, st0["kT"], wks, s, 8, st0["kevac"])
                    else:
                        emit_group(0, yT[:, 0:T], wqs, s, 0, st0["qevac"])

            tr1 = layernorm_round(0, after_tile=ln1_cb)
            ln1_done = [("sV", cnt["sV"])]
            ln_gate[0] = list(ln1_done)
            chain_guard[0] = ("sT", cnt["sT"])
            ln1_evac[0] = [("sV", cnt["sV"]), ("sS", cnt["sS"])]
            tap("xht1", lambda: xhT[:, :], [128, NCH * T], BF16, list(ln1_done))
            if STOP < 2:
                finish_debug()
            tap("vtok", lambda: vtok[:, :], [128, NT * D], BF16, [("sV", cnt["sV"])])
            if STOP < 3:
                finish_debug()
            vproj_done = ("sV", cnt["sV"])

            tok_wo = []
            for a in range(NCH):
                tok_wo.append(emit(
                    "gpsimd", lambda e, a=a: e.dma_start(
                        out=wbuf[:, D * a:D * (a + 1)], in_=wo_d[128 * a:128 * (a + 1), :]),
                    waits=[("sT", cnt["sT"])], inc="dW", amt=16))

            for stp in boundary_steps(0, with_groups=False):
                stp()
            if 'tok_wq' in globals_:
                tok_wq = globals_.pop('tok_wq')
                tok_wk = globals_.pop('tok_wk')

            for p in range(NPAIR):
                hA, hB = 2 * p, 2 * p + 1
                st_p = state[p]
                kevac = st_p["kevac"]
                qevac = st_p["qevac"]
                bias_tok_c = st_p["bias"]
                kTp = st_p["kT"]
                qvb = st_p["qv"]
                bias_col = 8 * (p % 2)
                if p < NPAIR - 1:
                    new_state(p + 1)
                    bsteps = boundary_steps(p + 1)
                else:
                    bsteps = []
                bq_i = [0]

                def run_bstep(cb):
                    if bq_i[0] < len(bsteps):
                        CB[0] = cb
                        bsteps[bq_i[0]]()
                        bq_i[0] += 1

                exp_tok = {}
                free_bank_tok = {}
                free_pt_tok = {}
                for c in range(4):
                    jlast = 4 * c + 3
                    bY = PB[4 + 2 * (c % 2)]
                    bL = PB[5 + 2 * (c % 2)]

                    def emit_scores(j, c=c):
                        off = max(0, 128 * j - 512 * c)
                        w = 512 - off
                        qlo = qvb + 512 * c + off
                        diag = (j >= 4 * c)
                        pab = PAB[j % 2]
                        sw = [qevac[c], kevac[j // 4]]
                        if p == 0 and c == 0 and j <= 1 and ln1_evac[0]:
                            sw += ln1_evac[0]
                        if (j % 2) in free_bank_tok:
                            sw.append(free_bank_tok[j % 2])
                        emit("tensor", lambda e, j=j, qlo=qlo, w=w, pab=pab, diag=diag,
                             kTp=kTp: e.matmul(
                            pab[:, 0:w], kTp[0:64, 128 * j:128 * (j + 1)],
                            yT[0:64, qlo:qlo + w], start=True, stop=not diag,
                            tile_position=(0, 0)), waits=sw, inc="sT")
                        emit("tensor", lambda e, j=j, qlo=qlo, w=w, pab=pab, diag=diag,
                             kTp=kTp: e.matmul(
                            pab[:, 512:512 + w], kTp[64:128, 128 * j:128 * (j + 1)],
                            yT[64:128, qlo:qlo + w], start=True, stop=not diag,
                            tile_position=(64, 0)), inc="sT")
                        if diag:
                            emit("tensor", lambda e, pab=pab: e.matmul(
                                pab[:, 0:128], cbf[:, C_I:C_I + 128],
                                cbf[:, C_R:C_R + 128], start=False, stop=True), inc="sT")
                            emit("tensor", lambda e, pab=pab: e.matmul(
                                pab[:, 512:640], cbf[:, C_I:C_I + 128],
                                cbf[:, C_R:C_R + 128], start=False, stop=True), inc="sT")
                        t_scores = ("sT", cnt["sT"])
                        slot = j % 2
                        ew = [t_scores, bias_tok_c[c]]
                        if slot in free_pt_tok:
                            ew.append(free_pt_tok[slot])
                        eA = emit("scalar", lambda e, slot=slot, w=w, pab=pab, c=c,
                                  bias_col=bias_col: e.activation(
                            pt[:, 1024 * slot:1024 * slot + 512 + w], pab[:, 0:512 + w],
                            AF.Exp, bias=biasMC[:, bias_col + c:bias_col + c + 1],
                            scale=0.125), waits=ew, inc="sS")
                        free_bank_tok[j % 2] = ("sS", eA[1])
                        exp_tok[j] = ("sS", eA[1])

                    def emit_lpv(j, c=c, bY=bY, bL=bL, jlast=jlast):
                        off = max(0, 128 * j - 512 * c)
                        w = 512 - off
                        pA = 1024 * (j % 2)
                        pB = 1024 * (j % 2) + 512
                        lw = [exp_tok[j], vproj_done]
                        if j == 0:
                            if (c % 2) in free_l:
                                lw.append(free_l[c % 2])
                            if (c % 2) in free_y:
                                lw.append(free_y[c % 2])
                            for bn in (4 + 2 * (c % 2), 5 + 2 * (c % 2)):
                                if bn in pbank_evac:
                                    lw.append(pbank_evac.pop(bn))
                        emit("tensor", lambda e, j=j, off=off, w=w, pA=pA, bL=bL:
                             e.matmul(
                            bL[0:1, off:512], cbf[:, C_ONE:C_ONE + 1],
                            pt[:, pA:pA + w],
                            start=(j == 0), stop=(j == jlast), tile_position=(0, 0)),
                            waits=lw, inc="sT")
                        emit("tensor", lambda e, j=j, off=off, w=w, pB=pB, bL=bL:
                             e.matmul(
                            bL[32:33, off:512], cbf[:, C_ONE:C_ONE + 1],
                            pt[:, pB:pB + w],
                            start=(j == 0), stop=(j == jlast), tile_position=(0, 32)),
                            inc="sT")
                        emit("tensor", lambda e, j=j, off=off, w=w, pA=pA, hA=hA,
                             bY=bY: e.matmul(
                            bY[0:64, off:512],
                            vtok[:, D * j + HD * hA: D * j + HD * hA + HD],
                            pt[:, pA:pA + w],
                            start=(j == 0), stop=(j == jlast), tile_position=(0, 0)),
                            inc="sT")
                        t_pv = emit("tensor", lambda e, j=j, off=off, w=w, pB=pB,
                                    hB=hB, bY=bY: e.matmul(
                            bY[64:128, off:512],
                            vtok[:, D * j + HD * hB: D * j + HD * hB + HD],
                            pt[:, pB:pB + w],
                            start=(j == 0), stop=(j == jlast), tile_position=(0, 64)),
                            inc="sT")
                        free_pt_tok[j % 2] = ("sT", t_pv[1])

                    for j in range(jlast + 2):
                        if j <= jlast:
                            emit_scores(j)
                        if pending_pre[0] is not None and j == 1:
                            pending_pre[0]()
                            pending_pre[0] = None
                        if pending_fin[0] is not None and j == (2 if c == 0 else 4):
                            pending_fin[0]()
                            pending_fin[0] = None
                        if ((c == 0 and j in (3, 4))
                                or (c == 1 and j in (5, 7))
                                or (c == 2 and j in (5, 7, 9, 11))
                                or (c == 3 and j in (5, 7, 9, 11, 13, 15, 16))):
                            run_bstep(6 if c % 2 == 0 else 4)
                        if j >= 1:
                            emit_lpv(j - 1)
                    att_last_pv = ("sT", cnt["sT"])
                    pv_last_c = att_last_pv

                    def fin_pre(c=c, bL=bL, pv_last_c=pv_last_c):
                        t_l1 = emit("vector", lambda e, bL=bL: e.tensor_copy(
                            rl[0:33, 0:512], bL[0:33, 0:512]), waits=[pv_last_c],
                            inc="sV")
                        emit("scalar", lambda e: e.activation(
                            rl[0:33, 0:512], rl[0:33, 0:512], AF.Ln,
                            bias=consts[0:33, 5:6], scale=1.0),
                            waits=[("sV", t_l1[1])], inc="sS")
                        t_rcp = emit("scalar", lambda e: e.activation(
                            rl[0:33, 0:512], rl[0:33, 0:512], AF.Exp,
                            bias=consts[0:33, 5:6], scale=-1.0), inc="sS", selfw=True)
                        fin_tok[0] = ("sS", t_rcp[1])
                        fin_tok[1] = ("sV", t_l1[1])

                    def fin(c=c, bY=bY, bL=bL, pv_last_c=pv_last_c, p=p):
                        bLn = 5 + 2 * (c % 2)
                        bYn = 4 + 2 * (c % 2)
                        rw = [fin_tok[0], fin_tok[1]]
                        if bLn in pbank_evac:
                            rw.append(pbank_evac[bLn])
                        emit("tensor", lambda e, bL=bL: e.matmul(
                            bL[0:64, 0:512], onesf[0:1, 0:64], rl[0:1, 0:512],
                            start=True, stop=True, tile_position=(0, 0)),
                            waits=rw, inc="sT")
                        t_rlb = emit("tensor", lambda e, bL=bL: e.matmul(
                            bL[64:128, 0:512], onesf[32:33, 0:64], rl[32:33, 0:512],
                            start=True, stop=True, tile_position=(32, 64)), inc="sT")
                        t_rlbe = emit("vector", lambda e, bL=bL: e.tensor_copy(
                            rlb[:, 0:512], bL[:, 0:512]), waits=[("sT", t_rlb[1])],
                            inc="sV")
                        t_ye = emit("vector", lambda e, p=p, c=c, bY=bY: e.tensor_mul(
                            yT[:, T * p + 512 * c: T * p + 512 * (c + 1)],
                            bY[:, 0:512], rlb[:, 0:512]),
                            waits=[pv_last_c], inc="sV", selfw=True)
                        pbank_evac[bLn] = ("sV", t_rlbe[1])
                        pbank_evac[bYn] = ("sV", t_ye[1])
                        free_l[c % 2] = ("sV", t_rlbe[1])
                        free_y[c % 2] = ("sV", t_ye[1])
                    pending_pre[0] = fin_pre
                    pending_fin[0] = fin
                while bq_i[0] < len(bsteps):
                    run_bstep(4)
                if 'tok_wq' in globals_:
                    tok_wq = globals_.pop('tok_wq')
                    tok_wk = globals_.pop('tok_wk')
                att_v_prev[0] = cnt["sV"]
                pair_exp_last[0] = ("sS", cnt["sS"])
            if pending_pre[0] is not None:
                pending_pre[0]()
                pending_pre[0] = None
            if pending_fin[0] is not None:
                pending_fin[0]()
                pending_fin[0] = None
            qkproj_last = qk_last[0]
            att_done_v = ("sV", cnt["sV"])
            att_done_t = ("sT", cnt["sT"])
            att_done_holder[0] = att_done_v
            att_sS_holder[0] = ("sS", cnt["sS"])
            tap("yt", lambda: yT[:, :], [128, NCH * T], BF16, [att_done_v])
            if STOP < 4:
                finish_debug()

            for i in range(NT):
                for h in range(2):
                    bnum = 6 + (2 * i + h) % 2
                    bank = PB[bnum]
                    bw = [att_done_v, ("dW", tok_wo[7][1])]
                    if bnum in pbank_evac:
                        bw.append(pbank_evac[bnum])
                    for a in range(NCH):
                        emit("tensor", lambda e, i=i, h=h, a=a, bank=bank: e.matmul(
                            bank[:, 0:512],
                            yT[:, T * a + 128 * i: T * a + 128 * (i + 1)],
                            wbuf[:, D * a + 512 * h: D * a + 512 * (h + 1)],
                            start=(a == 0), stop=False),
                            waits=bw if a == 0 else (), inc="sT")
                    t_mm = emit("tensor", lambda e, h=h, bank=bank: e.matmul(
                        bank[:, 0:512], cbf[32:33, C_ONE:C_ONE + 128],
                        brow3[32:33, 512 * h:512 * (h + 1)],
                        start=False, stop=True), inc="sT")
                    resid_tok[i] = emit("vector", lambda e, i=i, h=h, bank=bank:
                                        e.scalar_tensor_tensor(
                        xs[:, D * i + 512 * h: D * i + 512 * (h + 1)], bank[:, 0:512],
                        consts[:, 4:5], xs[:, D * i + 512 * h: D * i + 512 * (h + 1)],
                        op0=ALU.mult, op1=ALU.add), waits=[("sT", t_mm[1])], inc="sV")
                    pbank_evac[bnum] = ("sV", resid_tok[i][1])
            oproj_done_t = ("sT", cnt["sT"])
            oproj_v_holder[0] = ("sV", cnt["sV"])
            tap("xso", lambda: xs[:, :], [128, NT * D], F32,
                [("sV", resid_tok[NT - 1][1])])
            if STOP < 5:
                finish_debug()

            tok_wt = []
            for a in range(NCH):
                tok_wt.append(emit(
                    "gpsimd", lambda e, a=a: e.dma_start(
                        out=wbuf[:, D * a:D * (a + 1)], in_=wt_d[128 * a:128 * (a + 1), :]),
                    waits=[oproj_done_t], inc="dW", amt=16))
            tok_da = emit("gpsimd", lambda e: e.dma_start(out=qT[0:100, 0:1024], in_=da_d),
                          waits=[att_done_t, qkproj_last], inc="dW", amt=16)
            tok_upt = emit("gpsimd", lambda e: e.dma_start(out=qT[0:50, 1024:2048], in_=upt_d),
                           inc="dW", amt=16)
            tok_ldt = emit("gpsimd", lambda e: e.dma_start(out=pt[0:100, 0:4], in_=ldt_d),
                           inc="dW", amt=16)
            tok_lut = emit("gpsimd", lambda e: e.dma_start(out=pt[0:4, 4:54], in_=lut_d),
                           inc="dW", amt=16)

            tr2 = layernorm_round(1)
            ln2_done = [("sS", cnt["sS"]), ("sV", cnt["sV"])]
            tap("xht2", lambda: xhT[:, :], [128, NCH * T], BF16, list(ln2_done))

            for h in range(2):
                emit("tensor", lambda e, h=h: e.matmul(
                    pb6[0:4, 0:512], pt[0:100, 0:4], qT[0:100, 512 * h:512 * (h + 1)],
                    start=True, stop=True),
                    waits=[("dW", tok_lut[1]), ("sV", cnt["sV"])], inc="sT")
                emit("vector", lambda e, h=h: e.tensor_copy(
                    kT[0:4, 512 * h:512 * (h + 1)], pb6[0:4, 0:512]),
                    waits=[("sT", cnt["sT"])], inc="sV")
            for h in range(2):
                emit("tensor", lambda e, h=h: e.matmul(
                    pb6[0:50, 0:512], pt[0:4, 4:54], kT[0:4, 512 * h:512 * (h + 1)],
                    start=True, stop=True), waits=[("sV", cnt["sV"])], inc="sT")
                emit("vector", lambda e, h=h: e.tensor_copy(
                    kT[0:50, 1024 + 512 * h:1024 + 512 * (h + 1)], pb6[0:50, 0:512]),
                    waits=[("sT", cnt["sT"])], inc="sV")
            wefft_tok = [None] * NCH
            a2_tok = ("sV", cnt["sV"])
            for a in range(NCH):
                for h in range(2):
                    bank = PB[6 + (2 * a + h) % 2]
                    emit("tensor", lambda e, a=a, h=h, bank=bank: e.matmul(
                        bank[:, 0:512], kT[0:50, 1024 + 128 * a:1024 + 128 * (a + 1)],
                        qT[0:50, 1024 + 512 * h:1024 + 512 * (h + 1)],
                        start=True, stop=True),
                        waits=[a2_tok, ("sV", cnt["sV"])], inc="sT")
                    wefft_tok[a] = emit("vector", lambda e, a=a, h=h, bank=bank: e.tensor_add(
                        yT[:, D * a + 512 * h: D * a + 512 * (h + 1)], bank[:, 0:512],
                        wbuf[:, D * a + 512 * h: D * a + 512 * (h + 1)]),
                        waits=[("sT", cnt["sT"]), ("dW", tok_lut[1]), att_done_v],
                        inc="sV")

            tap("weff", lambda: yT[:, 0:NCH * D], [128, NCH * D], BF16,
                [("sV", wefft_tok[7][1])])

            ffn_resid = [None] * NT
            for i in range(NT):
                for h in range(2):
                    bnum = 6 + (2 * i + h) % 2
                    bank = PB[bnum]
                    bw = list(tr2[i]) + [("sV", wefft_tok[7][1])]
                    if bnum in pbank_evac:
                        bw.append(pbank_evac[bnum])
                    for a in range(NCH):
                        emit("tensor", lambda e, i=i, h=h, a=a, bank=bank: e.matmul(
                            bank[:, 0:512],
                            xhT[:, T * a + 128 * i: T * a + 128 * (i + 1)],
                            yT[:, D * a + 512 * h: D * a + 512 * (h + 1)],
                            start=(a == 0), stop=False),
                            waits=bw if a == 0 else (), inc="sT")
                    t_mm = emit("tensor", lambda e, h=h, bank=bank: e.matmul(
                        bank[:, 0:512], cbf[64:65, C_ONE:C_ONE + 128],
                        brow3[64:65, 512 * h:512 * (h + 1)],
                        start=False, stop=True), inc="sT")
                    ffn_resid[i] = emit("vector", lambda e, i=i, h=h, bank=bank:
                                        e.scalar_tensor_tensor(
                        xs[:, D * i + 512 * h: D * i + 512 * (h + 1)], bank[:, 0:512],
                        consts[:, 4:5], xs[:, D * i + 512 * h: D * i + 512 * (h + 1)],
                        op0=ALU.mult, op1=ALU.add), waits=[("sT", t_mm[1])], inc="sV")
                    pbank_evac[bnum] = ("sV", ffn_resid[i][1])

            for g in range(4):
                dst = y_d[512 * g:512 * (g + 1), :].rearrange("(i p) c -> p i c", p=128)
                src = xs[:, 4096 * g:4096 * (g + 1)].rearrange("p (i c) -> p i c", i=4)
                ws = [("sV", ffn_resid[4 * g + k][1]) for k in range(4)]
                emit("sync", lambda e, src=src, dst=dst: e.dma_start(out=dst, in_=src),
                     waits=ws, inc="dOut", amt=16)
            emit("sync", lambda e, v=cnt["dOut"] + 64: None, waits=[("dOut", cnt["dOut"])])

        except _Done:
            pass
        def replay(eng_obj, name):
            for ws, fn, inc, amt in plan[name]:
                for s, v in ws:
                    eng_obj.wait_ge(sems[s], v)
                ins = fn(eng_obj)
                if inc is not None:
                    ins.then_inc(sems[inc], amt)

        @block.tensor
        def _(t):
            replay(t, "tensor")

        @block.vector
        def _(v):
            replay(v, "vector")

        @block.scalar
        def _(s):
            replay(s, "scalar")

        @block.gpsimd
        def _(g):
            replay(g, "gpsimd")

        @block.sync
        def _(y):
            replay(y, "sync")

    return nc


def kernel(L0_attn_x, L0_attn_Wq, L0_attn_bq, L0_attn_Wk, L0_attn_bk,
           L0_attn_Wv, L0_attn_bv, L0_attn_Wo, L0_attn_bo,
           L0_ffn_W, L0_ffn_b, L0_ffn_lora_down, L0_ffn_lora_up,
           L0_ffn_down_aux, L0_ffn_up_aux):
    x = np.ascontiguousarray(np.asarray(L0_attn_x, np.float32))
    Wq = np.asarray(L0_attn_Wq, np.float32)
    Wk = np.asarray(L0_attn_Wk, np.float32)

    def dev_layout(W):
        return np.ascontiguousarray(
            W.reshape(8, 128, 8, 128).transpose(2, 1, 0, 3).reshape(1024, 1024))

    wq_dev = dev_layout(Wq)
    wk_dev = dev_layout(Wk)
    wv = np.ascontiguousarray(np.asarray(L0_attn_Wv, np.float32))
    wo = np.ascontiguousarray(np.asarray(L0_attn_Wo, np.float32))
    wt = np.ascontiguousarray(np.asarray(L0_ffn_W, np.float32).T)
    bq = np.asarray(L0_attn_bq, np.float32)
    bk = np.asarray(L0_attn_bk, np.float32)
    bqk = np.ascontiguousarray(np.concatenate(
        [bq.reshape(8, 128).T, bk.reshape(8, 128).T], axis=1))
    brow = np.ascontiguousarray(np.stack(
        [np.asarray(L0_attn_bv, np.float32), np.asarray(L0_attn_bo, np.float32),
         np.asarray(L0_ffn_b, np.float32)], axis=0))
    consts = np.zeros((128, 8), np.float32)
    consts[:, 0] = 1.0 / 1024
    consts[:, 1] = -1.0 / 1024
    consts[:, 2] = 1e-5
    consts[:, 3] = -0.125
    consts[:, 4] = 1.0
    consts[:, 5] = 0.0
    consts[:, 6] = -1.0
    idx = np.arange(128)
    cbf = np.zeros((128, 386), np.float32)
    cbf[idx, idx] = 1.0
    cbf[:, 128:256] = np.where(idx[:, None] > idx[None, :], -30000.0, 0.0)
    cbf[:, 256:384] = 1.0
    cbf[idx < 64, 384] = 1.0
    cbf[idx >= 64, 385] = 1.0
    cbf = cbf.astype(bfdt)
    ldt = np.ascontiguousarray(np.asarray(L0_ffn_lora_down, np.float32).T)
    lut = np.ascontiguousarray(np.asarray(L0_ffn_lora_up, np.float32).T)
    upt = np.ascontiguousarray(np.asarray(L0_ffn_up_aux, np.float32).T)
    da = np.ascontiguousarray(np.asarray(L0_ffn_down_aux, np.float32))

    shared = {"wq": wq_dev, "wk": wk_dev, "wv": wv, "wo": wo, "wt": wt,
              "bqk": bqk, "brow": brow, "consts": consts, "cbf": cbf,
              "ldt": ldt, "lut": lut, "upt": upt, "da": da}
    in_maps = [dict(shared, x=x[core]) for core in range(8)]

    nc = build_nc()
    res = run_bass_kernel_spmd(nc, in_maps, core_ids=list(range(8)))
    out = np.stack([np.asarray(res.results[i]["y"]) for i in range(8)], axis=0)
    return out.astype(np.float32)
